# version 10
# baseline (speedup 1.0000x reference)
import numpy as np
import ml_dtypes
import concourse.bass as bass
import concourse.mybir as mybir
from concourse.bass_utils import run_bass_kernel_spmd

F32 = mybir.dt.float32
BF16 = mybir.dt.bfloat16
U8 = mybir.dt.uint8
AF = mybir.ActivationFunctionType
ALU = mybir.AluOpType
BFNP = ml_dtypes.bfloat16

T = 2048
C = 1024
NCH = 8
L = 128
NCHUNK = T // L
HA = 4
I_FFN, E_FFN = 512, 8
DB = 64
NB = 2

PHASES = ("attn_a", "ffn1", "attn_b", "ffn2")
N_PHASES = 4
SERIAL = False


class Plan:
    def __init__(self, nc):
        self.nc = nc
        self.ops = []
        self.sems = {}
        self.count = {}

    def sem(self, name):
        if name not in self.sems:
            self.sems[name] = self.nc.alloc_semaphore(name)
            self.count[name] = 0
        return self.sems[name]

    def rec(self, engine, method, *args, waits=(), inc=None, incn=1, **kwargs):
        if SERIAL and method != "_wait_only":
            waits = list(waits) + [(n, self.count.get(n, 0))
                                   for n in ("t", "v", "s", "y", "g")]
        incs = []
        if inc is not None:
            self.sem(inc)
            self.count[inc] += incn
            incs.append((inc, incn))
        for w, _ in waits:
            self.sem(w)
        self.ops.append((engine, method, args, kwargs,
                         [(w, v) for (w, v) in waits if v > 0], incs))

    def now(self, name):
        return self.count.get(name, 0)

    def emit(self):
        nc, plan = self.nc, self

        def make(engine_name):
            def run(eng):
                wm = {}
                for (e, method, args, kwargs, waits, incs) in plan.ops:
                    if e != engine_name:
                        continue
                    for (w, v) in waits:
                        if wm.get(w, -1) < v:
                            eng.wait_ge(plan.sems[w], v)
                            wm[w] = v
                    if method == "_wait_only":
                        continue
                    inst = getattr(eng, method)(*args, **kwargs)
                    for (sname, n) in incs:
                        inst.then_inc(plan.sems[sname], n)
            return run

        with nc.Block() as blk:
            blk.tensor(make("tensor"))
            blk.vector(make("vector"))
            blk.scalar(make("scalar"))
            blk.sync(make("sync"))
            blk.gpsimd(make("gpsimd"))


def host_constants():
    tau = np.arange(L)
    masks = []
    levels = [32, 8, 2]
    for li, lsz in enumerate(levels):
        outer = L if li == 0 else levels[li - 1]
        for delta in (1, 2, 3):
            blk = tau // lsz
            valid = (blk[None, :] - blk[:, None] == delta) & \
                    (tau[None, :] // outer == tau[:, None] // outer)
            masks.append(valid.astype(np.uint8))
    valid = (tau[None, :] // 2 == tau[:, None] // 2) & (tau[None, :] >= tau[:, None])
    masks.append(valid.astype(np.uint8))
    maskcat = np.stack(masks).astype(np.uint8)

    pats = [np.arange(130, dtype=np.float64),
            np.asarray([0.0, -1.0])]
    flat = np.concatenate([np.asarray(q, np.float32) for q in pats])
    pat_tile = np.tile(flat[None, :], (128, 1)).astype(np.float32)
    offs, o = [], 0
    for q in pats:
        offs.append(o)
        o += len(q)
    return dict(masks=maskcat, pat=pat_tile, pat_offs=offs)


def dtsize(dt):
    return mybir.dt.size(dt)


class Builder:
    def __init__(self, n_phases=4, n_batches=NB, taps=()):
        self.nc = bass.Bass("TRN2", target_bir_lowering=False)
        self.p = Plan(self.nc)
        self.hc = host_constants()
        self.n_phases = n_phases
        self.n_batches = n_batches
        self.taps = set(taps)
        self.tap_t = {}
        self.bankmark = {}
        self._dram()
        self._sbuf()
        self._psum()

    def tap(self, name, ap, waits):
        if name not in self.taps:
            return
        shape = list(ap.shape)
        t = self.nc.dram_tensor("tap_" + name, shape, ap.dtype, kind="ExternalOutput").ap()
        self.tap_t[name] = t
        self.p.rec("sync", "dma_start", out=t[tuple(slice(None) for _ in shape)],
                   in_=ap, waits=waits, inc="y", incn=16,
                   allow_slow_non_contiguous=True)
        blk = [("y", self.p.now("y"))]
        self.p.rec("vector", "_wait_only", waits=blk)
        self.p.rec("scalar", "_wait_only", waits=blk)
        self.p.rec("tensor", "_wait_only", waits=blk)
        self.p.rec("gpsimd", "_wait_only", waits=blk)

    def _dram(self):
        nc = self.nc
        d = {}
        d["x"] = nc.dram_tensor("x", [NB, C, T], F32, kind="ExternalInput").ap()
        d["out"] = nc.dram_tensor("out", [NB, C, T], F32, kind="ExternalOutput").ap()
        for nm in ("wqT", "wkT", "wvT", "wgT", "woT"):
            d[nm] = nc.dram_tensor(nm, [C, C], F32, kind="ExternalInput").ap()
        d["a_small"] = nc.dram_tensor("a_small", [C, 6], F32, kind="ExternalInput").ap()
        d["ffn_w"] = nc.dram_tensor("ffn_w", [E_FFN, 3 * C * I_FFN], F32, kind="ExternalInput").ap()
        d["gate_wT"] = nc.dram_tensor("gate_wT", [C, E_FFN], F32, kind="ExternalInput").ap()
        d["e_bias_bc"] = nc.dram_tensor("e_bias_bc", [128, E_FFN], F32, kind="ExternalInput").ap()
        d["eidx_bc"] = nc.dram_tensor("eidx_bc", [128, E_FFN], F32, kind="ExternalInput").ap()
        d["bqkT"] = nc.dram_tensor("bqkT", [C, 2], F32, kind="ExternalInput").ap()
        d["wqkvT_b"] = nc.dram_tensor("wqkvT_b", [3, C, C], F32, kind="ExternalInput").ap()
        d["wo_b"] = nc.dram_tensor("wo_b", [C, C], F32, kind="ExternalInput").ap()
        d["bv_bc"] = nc.dram_tensor("bv_bc", [128, C], F32, kind="ExternalInput").ap()
        d["bo_b"] = nc.dram_tensor("bo_b", [C, 1], F32, kind="ExternalInput").ap()
        d["c_masks"] = nc.inline_tensor(self.hc["masks"].reshape(10, 128 * 128), name="c_masks").ap()
        d["c_pat"] = nc.inline_tensor(self.hc["pat"], name="c_pat").ap()
        d["c_ones16"] = nc.inline_tensor(np.ones((128, 128), BFNP), name="c_ones16").ap()
        d["c_ones32"] = nc.inline_tensor(np.ones((128, 128), np.float32), name="c_ones32").ap()
        d["c_id16"] = nc.inline_tensor(np.eye(128, dtype=BFNP), name="c_id16").ap()
        d["c_id32"] = nc.inline_tensor(np.eye(128, dtype=np.float32), name="c_id32").ap()
        d["c_zeros"] = nc.inline_tensor(np.zeros((128, 8), np.float32), name="c_zeros").ap()
        d["c_zeros16"] = nc.inline_tensor(np.zeros((128, 8), BFNP), name="c_zeros16").ap()
        sel = np.zeros((8, 1024), BFNP)
        for e in range(8):
            sel[e, e * 128:(e + 1) * 128] = 1
        d["c_sel"] = nc.inline_tensor(sel, name="c_sel").ap()
        d["c_n30"] = nc.inline_tensor(np.full((128, 1), -30.0, np.float32),
                                      name="c_n30").ap()
        self.d = d

    def _sbuf(self):
        nc = self.nc
        s = {}
        self._off = (nc.sbuf_base + 31) & ~31

        def at(name, shape, dtype, offset=None):
            per_part = int(np.prod(shape[1:])) * dtsize(dtype)
            per_part = (per_part + 31) & ~31
            if offset is None:
                offset = self._off
                self._off = offset + per_part
            s[name] = nc.alloc_sbuf_tensor_at(name, shape, dtype, offset=offset)
            return offset + per_part

        at("resid", [128, NCH, T], F32)
        xl0 = self._off
        at("x_ln", [128, NCH, T], BF16)
        gb0 = self._off
        at("gbuf", [128, NCH, T], BF16)
        wa0 = self._off
        self._off += 64 * 1024
        wa_end = self._off

        o = wa0
        o = at("w_q", [128, NCH, C], BF16, o)
        o = at("w_k", [128, NCH, C], BF16, o)
        o = at("w_v", [128, NCH, C], BF16, o)
        o = at("w_g", [128, NCH, C], BF16, o)
        assert o <= wa_end
        o = wa0
        o = at("w_ffn0", [128, 12288], BF16, o)
        o = at("w_ffn1", [128, 12288], BF16, o)
        o = at("w_gate", [128, NCH, E_FFN], BF16, o)
        o = at("f_sg", [128, 4, 512], BF16, o)
        o = at("sel16_sb", [8, 1024], BF16, o)
        o = at("cb_fm16", [8, 1024], BF16, o)
        o = at("cb_bc", [128, 2, 512], BF16, o)
        lg0 = o
        o = at("lg_sb", [8, 128], F32, o)
        at("gw_stg", [128, E_FFN], BF16, lg0)
        o = at("cb_tm", [128, NCHUNK, E_FFN], F32, o)
        o = at("lgT", [128, NCHUNK, E_FFN], F32, o)
        o = at("gw_sc", [128, NCHUNK, E_FFN], F32, o)
        o = at("gw_tmp", [128, NCHUNK, E_FFN], F32, o)
        o = at("gw_sm", [128, NCHUNK, E_FFN], F32, o)
        o = at("gw_msk", [128, NCHUNK, E_FFN], F32, o)
        o = at("gw_z", [128, NCHUNK, E_FFN], F32, o)
        o = at("gw_key", [128, NCHUNK, E_FFN], F32, o)
        o = at("gw_gs", [128, NCHUNK, 4], F32, o)
        o = at("gw_t1", [128, NCHUNK, 4], F32, o)
        o = at("gw_g2", [128, NCHUNK, 4], F32, o)
        o = at("gw_t2", [128, NCHUNK, 4], F32, o)
        o = at("gw_h2", [128, NCHUNK, 2], F32, o)
        o = at("gw_m1", [128, NCHUNK, 1], F32, o)
        o = at("gw_m2", [128, NCHUNK, 1], F32, o)
        o = at("gw_se", [128, NCHUNK, 1], F32, o)
        o = at("gw_rs", [128, NCHUNK, 1], F32, o)
        assert o <= wa_end, (o, wa_end)
        o = wa0
        o = at("wb_q", [128, NCH, 256], BF16, o)
        o = at("wb_k", [128, NCH, 256], BF16, o)
        o = at("wb_v", [128, NCH, 256], BF16, o)
        o = at("wb_o", [128, NCH, C], BF16, o)
        o = at("qf_b", [128, 2, T], BF16, o)
        o = at("kf_b", [128, 2, T], BF16, o)
        o = at("vt_b", [128, NCHUNK, 260], BF16, o)
        o = at("exps", [128, 8, 256], BF16, o)
        o = at("b_rz", [128, 512], F32, o)
        o = at("b_rzbc", [128, 512], F32, o)
        o = at("att_tmp", [128, 512], BF16, o)
        assert o <= wa_end, (o, wa_end)

        o = gb0
        o = at("w_o_a", [128, NCH, C], BF16, o)
        o = at("a_A", [128, NCH, 257], F32, o)
        o = at("a_g", [128, NCH, L], F32, o)
        assert o <= wa0, (o, wa0)
        at("att_b", [128, NCH, T], BF16, gb0)
        at("f_y", [128, NCH, 1024], F32, gb0)

        o = xl0
        o = at("warm", [128, 512], F32, o)
        o = at("a_qexp", [128, NCH, L], BF16, xl0)
        o = at("a_kexp", [128, NCH, L], BF16, o)
        o = at("a_vaug", [128, HA, 257], BF16, o)
        o = at("a_var", [128, 3, NCH, L], BF16, o)
        o = at("a_Abf", [128, 2, 257], BF16, o)
        o = at("a_kutm", [128, 2, L], BF16, o)
        o = at("a_att", [128, NCH, L], BF16, o)
        o = at("a_ku", [128, NCH, L], BF16, o)
        o = at("t_pow", [128, NCH, 130], BF16, o)
        o = at("xw16", [128, 2, NCH, 132], BF16, o)
        o = at("xsq16", [128, NCH, 132], BF16, o)
        o = at("xlnw", [128, NCH, 132], BF16, o)
        o = at("mix_s", [128, 1, NCH, L], BF16, o)
        o = at("mixd", [128, NCH, L], BF16, o)
        assert o <= gb0, (o, gb0)

        at("c_ones16_sb", [128, 128], BF16)
        at("c_ones32_sb", [128, 128], F32)
        at("c_id16_sb", [128, 128], BF16)
        at("c_id32_sb", [128, 128], F32)
        at("c_zeros_sb", [128, 8], F32)
        at("c_masks_sb", [128, 10, 128], U8)
        at("c_pat_sb", [128, 132], F32)
        at("a_small_sb", [128, NCH, 6], F32)
        at("bqk_sb", [128, NCH, 2], F32)
        at("bo_sb", [128, NCH, 1], F32)
        at("bv_bc_sb", [128, C], F32)
        at("e_bias_sb", [128, E_FFN], F32)
        at("eidx_sb", [128, E_FFN], F32)
        at("ln_m", [128, 132], F32)
        at("ln_r", [128, 132], F32)
        at("scr8", [128, NCH, 132], F32)
        at("a_pt", [128, 4, 128], BF16)
        at("rz", [1, 128], F32)
        at("rzbc", [128, 128], F32)
        at("f_cbT", [128, E_FFN], F32)
        at("f_sc", [128, E_FFN], F32)
        at("f_tmp", [128, E_FFN], F32)
        at("f_tmp2", [128, E_FFN], F32)
        at("f_gs", [128, 4], F32)
        at("f_key", [128, E_FFN], F32)
        at("rmsq", [128, 128], F32)
        at("c_n30_sb", [128, 1], F32)
        at("t_dk", [128, NCH, 2], BF16)
        at("t_logw", [128, NCH], F32)
        at("t_wL", [128, NCH], F32)
        assert self._off <= 229376, self._off
        self.s = s

    def _psum(self):
        nc = self.nc
        self.ps = [nc.alloc_psum_tensor(f"ps{i}", [128, 512], F32) for i in range(8)]
        self.ps16 = self.ps[7].ap().bitcast(BF16) if False else None

    def V(self, method, *a, waits=(), **k):
        self.p.rec("vector", method, *a, waits=waits, inc="v", **k)
        return ("v", self.p.now("v"))

    def S(self, method, *a, waits=(), **k):
        self.p.rec("scalar", method, *a, waits=waits, inc="s", **k)
        return ("s", self.p.now("s"))

    def MM(self, *a, waits=(), inc_group=False, **k):
        self.p.rec("tensor", "matmul", *a, waits=waits,
                   inc="t" if inc_group else None, **k)
        return ("t", self.p.now("t"))

    def TR(self, *a, waits=(), inc_group=False, **k):
        self.p.rec("tensor", "transpose", *a, waits=waits,
                   inc="t" if inc_group else None, **k)
        return ("t", self.p.now("t"))

    def bank_w(self, i):
        return [self.bankmark[i]] if i in self.bankmark else []

    def bank_done(self, i, mark):
        self.bankmark[i] = mark

    def gen_consts(self):
        p, d, s = self.p, self.d, self.s
        p.rec("sync", "dma_start", out=s["c_ones16_sb"][:], in_=d["c_ones16"][:, :], inc="y", incn=16)
        p.rec("sync", "dma_start", out=s["c_ones32_sb"][:], in_=d["c_ones32"][:, :], inc="y", incn=16)
        p.rec("sync", "dma_start", out=s["c_id16_sb"][:], in_=d["c_id16"][:, :], inc="y", incn=16)
        p.rec("sync", "dma_start", out=s["c_id32_sb"][:], in_=d["c_id32"][:, :], inc="y", incn=16)
        p.rec("sync", "dma_start", out=s["c_zeros_sb"][:], in_=d["c_zeros"][:, :], inc="y", incn=16)
        p.rec("sync", "dma_start", out=s["c_masks_sb"][:, :, :],
              in_=d["c_masks"][:, :].rearrange("a (p b) -> p a b", p=128), inc="y", incn=16)
        p.rec("sync", "dma_start", out=s["c_pat_sb"][:], in_=d["c_pat"][:, :], inc="y", incn=16)
        p.rec("sync", "dma_start", out=s["a_small_sb"][:, :, :],
              in_=d["a_small"][:, :].rearrange("(j p) k -> p j k", p=128), inc="y", incn=16,
              allow_slow_non_contiguous=True)
        p.rec("sync", "dma_start", out=s["bqk_sb"][:, :, :],
              in_=d["bqkT"][:, :].rearrange("(j p) k -> p j k", p=128), inc="y", incn=16,
              allow_slow_non_contiguous=True)
        p.rec("sync", "dma_start", out=s["bo_sb"][:, :, :],
              in_=d["bo_b"][:, :].rearrange("(j p) k -> p j k", p=128), inc="y", incn=16,
              allow_slow_non_contiguous=True)
        p.rec("sync", "dma_start", out=s["bv_bc_sb"][:], in_=d["bv_bc"][:, :], inc="y", incn=16)
        p.rec("sync", "dma_start", out=s["c_n30_sb"][:], in_=d["c_n30"][:, :], inc="y", incn=16)
        p.rec("sync", "dma_start", out=s["e_bias_sb"][:], in_=d["e_bias_bc"][:, :], inc="y", incn=16)
        p.rec("sync", "dma_start", out=s["eidx_sb"][:], in_=d["eidx_bc"][:, :], inc="y", incn=16)
        self.cld = ("y", p.now("y"))

    def gen_warmup(self):
        s = self.s
        w = [self.cld]
        wsrc = s["c_ones32_sb"][:, 0:128].unsqueeze(1).broadcast_to([128, 4, 128])
        for i in range(16):
            self.V("tensor_copy", s["warm"][:, 0:512].rearrange("p (a b) -> p a b", b=128),
                   wsrc, waits=w if i == 0 else ())
        mv = self.V("tensor_copy", s["warm"][:, 0:128], s["c_ones32_sb"][:])
        for i in range(4):
            ms = self.S("activation", s["warm"][:, 128:256], s["c_ones32_sb"][:], AF.Exp,
                        waits=w if i == 0 else ())
        for i in range(6):
            mt = self.MM(self.ps[0][:, 0:128], s["c_ones16_sb"][:], s["c_ones16_sb"][:],
                         start=True, stop=True, waits=w if i == 0 else (),
                         inc_group=(i == 5))
        mg = ("g", self.p.now("g"))
        bar = [mv, ms, mt, self.cld]
        for e in ("vector", "scalar", "tensor", "sync", "gpsimd"):
            self.p.rec(e, "_wait_only", waits=bar)
        self.bank_done(0, mv)

    def gen_tables(self):
        p, s = self.p, self.s
        w = [self.cld, ("v", p.now("v")), ("t", p.now("t"))]
        self.S("activation", s["t_logw"][:, :].unsqueeze(2), s["a_small_sb"][:, :, 0:1],
               AF.Exp, waits=w)
        mk = self.V("tensor_scalar_mul", s["t_logw"][:, :], s["t_logw"][:, :], -1.0,
                    waits=[("s", p.now("s"))])
        self.S("activation", s["t_wL"][:, :].unsqueeze(2), s["t_logw"][:, :].unsqueeze(2),
               AF.Exp, scale=float(L), waits=[mk])
        offs = self.hc["pat_offs"]
        logw_b = s["t_logw"][:, :].unsqueeze(2)

        def build(dst_base, pat_off, plen):
            done = 0
            while done < plen:
                n = min(128, plen - done)
                pat = s["c_pat_sb"][:, pat_off + done:pat_off + done + n] \
                    .unsqueeze(1).broadcast_to([128, NCH, n])
                lw = logw_b.broadcast_to([128, NCH, n])
                mv = self.V("tensor_tensor", s["scr8"][:, :, 0:n], pat, lw, ALU.mult,
                            waits=[("s", p.now("s"))])
                self.S("activation", dst_base[:, :, done:done + n], s["scr8"][:, :, 0:n],
                       AF.Exp, waits=[mv])
                done += n

        build(s["t_pow"], offs[0], 130)
        build(s["t_dk"], offs[1], 2)
        self.tables_done = ("s", p.now("s"))

    def tpow(self, lo, hi, rev=False):
        if not rev:
            return self.s["t_pow"][:, :, lo:hi]
        return self.s["t_pow"][:, :, hi - 1:(None if lo == 0 else lo - 1):-1]

    def gen_load_x(self, batch, waits=()):
        p, d, s = self.p, self.d, self.s
        p.rec("sync", "dma_start", out=s["resid"][:, :, :],
              in_=d["x"][batch].rearrange("(j p) t -> p j t", p=128),
              waits=waits, inc="y", incn=16)
        self.xld = ("y", p.now("y"))

    def gen_attn_a_weights(self, waits=()):
        p, d, s = self.p, self.d, self.s
        for nm, dst in (("wqT", "w_q"), ("wkT", "w_k"), ("wvT", "w_v"), ("wgT", "w_g")):
            p.rec("gpsimd", "dma_start", out=s[dst][:, :, :],
                  in_=d[nm][:, :].rearrange("(j p) o -> p j o", p=128),
                  waits=waits, inc="g", incn=16)
        p.rec("gpsimd", "dma_start", out=s["w_o_a"][:, :, :],
              in_=d["woT"][:, :].rearrange("(j p) o -> p j o", p=128),
              waits=waits, inc="g", incn=16)
        self.aw = ("g", p.now("g"))

    def gen_attn_a(self, batch):
        p, s, d = self.p, self.s, self.d
        wts = [self.xld, self.aw, self.tables_done,
               getattr(self, "vaug_cons", ("t", 0)), ("v", p.now("v"))]
        for h in range(HA):
            p.rec("sync", "dma_start", out=s["a_vaug"][:, h, 256:257],
                  in_=d["c_ones16"][:, 0:1], waits=wts if h == 0 else (),
                  inc="y", incn=16, allow_slow_non_contiguous=True)
        p.rec("sync", "dma_start", out=s["xw16"][:, 0, :, 0:1],
              in_=d["c_zeros16"][:, 0:NCH].unsqueeze(2), inc="y", incn=16,
              allow_slow_non_contiguous=True)
        self.vaug_ones = ("y", p.now("y"))
        self.prev0 = ("y", p.now("y"))
        self.win_mark = self.V("tensor_copy", s["xw16"][:, 0, :, 1:129],
                               s["resid"][:, :, 0:L],
                               waits=[self.xld, self.prev0, self.tables_done, self.aw,
                                      getattr(self, "attA", ("v", 0))])
        self.Aup = ("t", 0)
        self.attA = ("v", 0) if batch == 0 else self.attA
        self.var_marks = [None] * 3
        self.var_cons = [("t", 0)] * 3
        self.var_i = 0
        self.mix_cons = [("t", 0)] * 1
        for c in range(NCHUNK):
            self._attn_a_chunk(batch, c)

    def _emit_var(self, kind, idx, trim, src_mark):
        p, s = self.p, self.s
        if kind in ("lk", "dk"):
            slot = 2
        else:
            slot = self.var_i % 2
            self.var_i += 1
        var = s["a_var"]
        waits = [src_mark, self.var_cons[slot]]
        if kind == "q1":
            tab = self.tpow(1, 129)
            mk = self.V("tensor_mul", var[:, slot, :, :], s["a_qexp"][:, :, :], tab,
                        waits=waits)
        elif kind == "lq":
            lsz = (32, 32, 32, 8, 8, 8, 2, 2, 2)[idx]
            delta = (1, 2, 3)[idx % 3]
            nblk = (L - trim) // lsz
            lo = (delta - 1) * lsz
            tab = self.tpow(lo, lo + lsz).unsqueeze(2).broadcast_to([128, NCH, nblk, lsz])
            srcv = s["a_qexp"][:, :, trim:].rearrange("p j (a b) -> p j a b", b=lsz)
            dst = var[:, slot, :, 0:L - trim].rearrange("p j (a b) -> p j a b", b=lsz)
            mk = self.V("tensor_mul", dst, srcv, tab, waits=waits)
        elif kind == "dq":
            tab = self.tpow(0, 2).unsqueeze(2).broadcast_to([128, NCH, 64, 2])
            srcv = s["a_qexp"][:, :, :].rearrange("p j (a b) -> p j a b", b=2)
            dst = var[:, slot, :, :].rearrange("p j (a b) -> p j a b", b=2)
            mk = self.V("tensor_mul", dst, srcv, tab, waits=waits)
        elif kind == "lk":
            lsz = (32, 8, 2)[idx]
            tab = self.tpow(1, lsz + 1, rev=True).unsqueeze(2).broadcast_to([128, NCH, L // lsz, lsz])
            srcv = s["a_kexp"][:, :, :].rearrange("p j (a b) -> p j a b", b=lsz)
            dst = var[:, slot, :, :].rearrange("p j (a b) -> p j a b", b=lsz)
            mk = self.V("tensor_mul", dst, srcv, tab, waits=waits)
        elif kind == "dk":
            tab = s["t_dk"][:, :, :].unsqueeze(2).broadcast_to([128, NCH, 64, 2])
            srcv = s["a_kexp"][:, :, :].rearrange("p j (a b) -> p j a b", b=2)
            dst = var[:, slot, :, :].rearrange("p j (a b) -> p j a b", b=2)
            mk = self.V("tensor_mul", dst, srcv, tab, waits=waits)
        self.var_marks[slot] = mk
        return slot

    def _attn_a_chunk(self, batch, c):
        p, s, ps = self.p, self.s, self.ps
        self.tap(f"wg_at_c{c}", s["w_g"][:, :, :], [self.aw, ("v", p.now("v")), ("t", p.now("t")), ("s", p.now("s"))])
        ones16 = s["c_ones16_sb"]
        t0 = c * L
        wlen = 129
        slot_w = c % 2
        xw = s["xw16"][:, slot_w, :, :]

        if c + 1 < NCHUNK:
            nxt = s["xw16"][:, (c + 1) % 2, :, :]
            self.win_mark_next = self.V(
                "tensor_copy", nxt[:, :, 0:129],
                s["resid"][:, :, (c + 1) * L - 1:(c + 1) * L + L],
                waits=[self.win_mark, self.attA])

        m_xsq = self.V("tensor_mul", s["xsq16"][:, :, 0:wlen], xw[:, :, 0:wlen],
                       xw[:, :, 0:wlen], waits=[self.win_mark])
        for j in range(NCH):
            self.MM(ps[2][:, 0:wlen], ones16[:], xw[:, j, 0:wlen],
                    start=(j == 0), stop=(j == NCH - 1),
                    waits=([self.win_mark] + self.bank_w(2)) if j == 0 else ())
        for j in range(NCH):
            t_ln = self.MM(ps[2][:, 136:136 + wlen], ones16[:], s["xsq16"][:, j, 0:wlen],
                           start=(j == 0), stop=(j == NCH - 1),
                           waits=[m_xsq] if j == 0 else (), inc_group=(j == NCH - 1))
        self.V("tensor_scalar_mul", s["ln_m"][:, 0:wlen], ps[2][:, 0:wlen], 1.0 / C,
               waits=[t_ln])
        self.V("tensor_mul", s["ln_r"][:, 0:wlen], s["ln_m"][:, 0:wlen], s["ln_m"][:, 0:wlen])
        m_var = self.V("scalar_tensor_tensor", s["ln_r"][:, 0:wlen], ps[2][:, 136:136 + wlen],
                       1.0 / C, s["ln_r"][:, 0:wlen], ALU.mult, ALU.subtract)
        m_var = self.V("tensor_scalar_add", s["ln_r"][:, 0:wlen], s["ln_r"][:, 0:wlen], 1e-5)
        self.S("activation", s["ln_r"][:, 0:wlen], s["ln_r"][:, 0:wlen], AF.Ln,
               waits=[m_var])
        m_rstd = self.S("activation", s["ln_r"][:, 0:wlen], s["ln_r"][:, 0:wlen], AF.Exp,
                        scale=-0.5)
        mb = s["ln_m"][:, 0:wlen].unsqueeze(1).broadcast_to([128, NCH, wlen])
        rb = s["ln_r"][:, 0:wlen].unsqueeze(1).broadcast_to([128, NCH, wlen])
        self.V("tensor_sub", s["scr8"][:, :, 0:wlen], xw[:, :, 0:wlen], mb,
               waits=[m_rstd])
        m_xln = self.V("tensor_mul", s["xlnw"][:, :, 0:wlen], s["scr8"][:, :, 0:wlen], rb)
        self.bank_done(2, m_var)
        self.tap(f"xlnw_c{c}", s["xlnw"][:, :, 1:129], [m_xln])
        self.tap(f"xlnw0_c{c}", s["xlnw"][:, :, 0:1], [m_xln])
        self.tap(f"win_c{c}", xw[:, :, 0:129], [self.win_mark if c + 1 < NCHUNK else m_xln])
        self.win_mark = getattr(self, "win_mark_next", None)

        xln1 = s["xlnw"][:, :, 1:129]
        xln0 = s["xlnw"][:, :, 0:128]
        m_d = self.V("tensor_sub", s["mixd"][:, :, :], xln1, xln0, waits=[m_xln])

        marks = {}
        projs = [("q", "w_q", 1, (0, 1)), ("k", "w_k", 2, (3, 4)),
                 ("v", "w_v", 3, (0, 1)), ("g", "w_g", 4, (3, 4))]
        for pi, (nm, wname, mix_i, (bA, bB)) in enumerate(projs):
            slot_i = 0
            slot = s["mix_s"][:, 0, :, :]
            mcol = s["a_small_sb"][:, :, mix_i:mix_i + 1].broadcast_to([128, NCH, L])
            self.V("tensor_mul", slot, s["mixd"][:, :, :], mcol,
                   waits=[m_d, self.mix_cons[slot_i]])
            m_mix = self.V("tensor_add", slot, slot, xln0)
            self.tap(f"mix{nm}_c{c}", slot, [m_mix])
            wgt = s[wname]
            if nm != "v":
                for m in range(8):
                    bank = ps[bA] if m < 4 else ps[bB]
                    cols = (m % 4) * 128
                    for kk in range(NCH):
                        t_mk = self.MM(bank[:, cols:cols + 128],
                                       wgt[:, kk, m * 128:(m + 1) * 128], slot[:, kk, :],
                                       start=(kk == 0), stop=(kk == NCH - 1),
                                       waits=([m_mix] + self.bank_w(bA) + self.bank_w(bB))
                                       if (m == 0 and kk == 0) else (),
                                       inc_group=(m == 7 and kk == NCH - 1))
            else:
                for half in range(2):
                    bank = ps[bA] if half == 0 else ps[bB]
                    for kk in range(NCH):
                        t_mk = self.MM(bank[:, 0:512],
                                       slot[:, kk, :], wgt[:, kk, half * 512:(half + 1) * 512],
                                       start=(kk == 0), stop=(kk == NCH - 1),
                                       waits=([m_mix] + self.bank_w(bA) + self.bank_w(bB))
                                       if (half == 0 and kk == 0) else (),
                                       inc_group=(half == 1 and kk == NCH - 1))
            self.mix_cons[slot_i] = t_mk
            marks[nm] = (t_mk, bA, bB)

            if nm in ("q", "k"):
                dst = s["a_qexp"] if nm == "q" else s["a_kexp"]
                self.V("tensor_copy", dst[:, 0:4, :].rearrange("p a b -> p (a b)"),
                       ps[bA][:, 0:512], waits=[t_mk])
                m_ev = self.V("tensor_copy", dst[:, 4:8, :].rearrange("p a b -> p (a b)"),
                              ps[bB][:, 0:512])
                self.bank_done(bA, m_ev)
                self.bank_done(bB, m_ev)
                m_sq = self.V("tensor_mul", s["xsq16"][:, :, 0:L], dst[:, :, :], dst[:, :, :])
                for kk in range(NCH):
                    t_rms = self.MM(ps[2][:, 272:400], ones16[:], s["xsq16"][:, kk, 0:L],
                                    start=(kk == 0), stop=(kk == NCH - 1),
                                    waits=([m_sq] + self.bank_w(2)) if kk == 0 else (),
                                    inc_group=(kk == NCH - 1))
                self.S("activation", s["rmsq"][:], ps[2][:, 272:400], AF.Ln, scale=1.0 / C,
                       waits=[t_rms])
                m_rms = self.S("activation", s["rmsq"][:], s["rmsq"][:], AF.Exp, scale=0.5)
                self.V("tensor_scalar_add", s["rmsq"][:], s["rmsq"][:], 1e-8, waits=[m_rms])
                m_rec = self.V("reciprocal", s["rmsq"][:], s["rmsq"][:])
                self.bank_done(2, m_rec)
                rbc = s["rmsq"][:].unsqueeze(1).broadcast_to([128, NCH, L])
                self.V("tensor_mul", s["scr8"][:, :, 0:L], dst[:, :, :], rbc,
                       waits=[marks["q_exp"]] if nm == "k" else ())
                nscb = s["a_small_sb"][:, :, 5:6].broadcast_to([128, NCH, L])
                m_qr = self.V("tensor_mul", s["scr8"][:, :, 0:L], s["scr8"][:, :, 0:L], nscb)
                m_exp = self.S("activation", dst[:, :, :], s["scr8"][:, :, 0:L], AF.Exp,
                               waits=[m_qr])
                marks[nm + "_exp"] = m_exp
                self.tap(f"{nm}exp_c{c}", dst[:, :, :], [m_exp])
            elif nm == "v":
                self.V("tensor_copy",
                       s["a_vaug"][:, 0:2, 0:256],
                       ps[bA][:, 0:512].rearrange("p (a b) -> p a b", b=256),
                       waits=[t_mk, getattr(self, "vaug_cons", ("t", 0))])
                m_ev = self.V("tensor_copy",
                              s["a_vaug"][:, 2:4, 0:256],
                              ps[bB][:, 0:512].rearrange("p (a b) -> p a b", b=256))
                self.bank_done(bA, m_ev)
                self.bank_done(bB, m_ev)
                marks["vaug"] = m_ev
            else:
                if f"gps_c{c}" in self.taps:
                    mdbg = self.V("tensor_copy", s["warm"][:, 0:512], ps[bA][:, 0:512],
                                  waits=[t_mk])
                    self.tap(f"gps_c{c}", s["warm"][:, 0:512], [mdbg])
                self.S("activation", s["a_g"][:, 0:4, :],
                       ps[bA][:, 0:512].rearrange("p (a b) -> p a b", b=L), AF.Sigmoid,
                       waits=[t_mk, ("v", p.now("v"))])
                m_g = self.S("activation", s["a_g"][:, 4:8, :],
                             ps[bB][:, 0:512].rearrange("p (a b) -> p a b", b=L),
                             AF.Sigmoid)
                self.bank_done(bA, m_g)
                self.bank_done(bB, m_g)
                marks["g"] = m_g
                self.tap(f"g_c{c}", s["a_g"][:, :, :], [m_g])
                self.tap(f"vaug_c{c}", s["a_vaug"][:, :, :], [marks["vaug"]])
                self.tap(f"xlnw_c{c}", s["xlnw"][:, :, :], [m_xln])

        pt_cons = getattr(self, "pt_cons", [("t", 0)] * 4)
        zb = s["c_zeros_sb"][:, 0:1].unsqueeze(1).broadcast_to([128, 4, 128])
        m_ptz = self.V("tensor_copy", s["a_pt"][:, :, :], zb, waits=pt_cons)

        ev_marks = {h: [] for h in range(HA)}
        piece_i = 0
        piece_plan = []
        for level_i, lsz in enumerate((32, 8, 2)):
            piece_plan.append(("lk", level_i, 0, None))
            for di, delta in enumerate((1, 2, 3)):
                piece_plan.append(("lq", level_i * 3 + di, delta * lsz, piece_i))
                piece_i += 1
        piece_plan.append(("dk", 0, 0, None))
        piece_plan.append(("dq", 0, 0, 9))

        kslot = None
        for (kind, idx, trim, pidx) in piece_plan:
            if kind in ("lk", "dk"):
                kslot = self._emit_var(kind, idx, 0, marks["k_exp"])
                self.tap(f"var_{kind}{idx}_c{c}", s["a_var"][:, kslot, :, :],
                         [self.var_marks[kslot]])
                continue
            qslot = self._emit_var(kind, idx, trim, marks["q_exp"])
            self.tap(f"var_{kind}{idx}_c{c}", s["a_var"][:, qslot, :, :],
                     [self.var_marks[qslot]])
            N = L - trim
            for h in range(HA):
                bank = 5 if (pidx * HA + h) % 2 == 0 else 6
                for kk in range(2):
                    t_p = self.MM(ps[bank][:, 0:N],
                                  s["a_var"][:, kslot, 2 * h + kk, :],
                                  s["a_var"][:, qslot, 2 * h + kk, 0:N],
                                  start=(kk == 0), stop=(kk == 1),
                                  waits=([self.var_marks[kslot], self.var_marks[qslot]]
                                         + self.bank_w(bank)) if kk == 0 else (),
                                  inc_group=(kk == 1))
                self.var_cons[kslot] = t_p
                self.var_cons[qslot] = t_p
                m_cp = self.V("copy_predicated", s["a_pt"][:, h, trim:trim + N],
                              s["c_masks_sb"][:, piece_def_idx(pidx), trim:trim + N]
                              if False else s["c_masks_sb"][:, pidx, trim:trim + N],
                              ps[bank][:, 0:N],
                              waits=[t_p, m_ptz])
                self.bank_done(bank, m_cp)
                ev_marks[h].append(m_cp)

        q1slot = self._emit_var("q1", 0, 0, marks["q_exp"])
        m_ku = self.V("tensor_mul", s["a_ku"][:, :, :], s["a_kexp"][:, :, :],
                      self.tpow(0, L, rev=True),
                      waits=[marks["k_exp"], getattr(self, "ku_cons", ("t", 0))])

        att_marks = []
        pt_next = []
        if not hasattr(self, "_rz_cons"):
            self._rz_cons = {}
            self._head_ts = ("t", 0)
        hstate = {}
        SKEW = True
        if SKEW:
            for step in range(HA + 1):
                if step < HA:
                    hstate[step] = self._head_p1(c, step, q1slot, ev_marks[step],
                                                 marks)
                if step >= 1:
                    m_att, m_ptfree = self._head_p2(c, step - 1, hstate[step - 1],
                                                    marks, m_ku)
                    att_marks.append(m_att)
                    pt_next.append(m_ptfree)
        else:
            for h in range(HA):
                st = self._head_p1(c, h, q1slot, ev_marks[h], marks)
                m_att, m_ptfree = self._head_p2(c, h, st, marks, m_ku)
                att_marks.append(m_att)
                pt_next.append(m_ptfree)
        self.pt_cons = pt_next
        self.tap(f"A_c{c}", s["a_A"][:, :, :], [self.Aup])

        for m in range(8):
            bank = 5 if m < 4 else 6
            cols = (m % 4) * 128
            for kk in range(NCH):
                t_wo = self.MM(ps[bank][:, cols:cols + 128],
                               s["w_o_a"][:, kk, m * 128:(m + 1) * 128], s["a_att"][:, kk, :],
                               start=(kk == 0), stop=(kk == NCH - 1),
                               waits=(att_marks + self.bank_w(5) + self.bank_w(6))
                               if (m == 0 and kk == 0) else (),
                               inc_group=(m == 7 and kk == NCH - 1))
        self.V("tensor_add", s["resid"][:, 0:4, t0:t0 + L],
               s["resid"][:, 0:4, t0:t0 + L],
               ps[5][:, 0:512].rearrange("p (a b) -> p a b", b=L), waits=[t_wo])
        m_res = self.V("tensor_add", s["resid"][:, 4:8, t0:t0 + L],
                       s["resid"][:, 4:8, t0:t0 + L],
                       ps[6][:, 0:512].rearrange("p (a b) -> p a b", b=L))
        self.bank_done(5, m_res)
        self.bank_done(6, m_res)
        self.attA = m_res

    def _head_p1(self, c, h, q1slot, ev_marks_h, marks):
        p, s, ps = self.p, self.s, self.ps
        first = (c == 0)
        b0 = 0 if h % 2 == 0 else 2
        bbc = 1 if h % 2 == 0 else 5
        SA = ps[b0][:, 0:128]
        SB = ps[b0][:, 128:256]
        SZ = ps[b0][0:1, 256:384]
        rz_ap = s["rz"][0:1, 0:128] if h % 2 == 0 else s["scr8"][0:1, 2, 0:128]
        rzbc_ap = s["rzbc"][:] if h % 2 == 0 else s["scr8"][:, 1, 0:128]
        var = s["a_var"]
        w0 = [self.var_marks[q1slot]] + self.bank_w(b0)
        if not first:
            m_Abf = self.V("tensor_copy", s["a_Abf"][:, :, :],
                           s["a_A"][:, 2 * h:2 * h + 2, :],
                           waits=[self.Aup, self._head_ts])
            w0 = w0 + [m_Abf]
            for m in range(2):
                dst = SA if m == 0 else SB
                for kk in range(2):
                    self.MM(dst, s["a_Abf"][:, kk, m * 128:(m + 1) * 128],
                            var[:, q1slot, 2 * h + kk, :],
                            start=(kk == 0), stop=False,
                            waits=w0 if (m == 0 and kk == 0) else ())
            for kk in range(2):
                self.MM(SZ, s["a_Abf"][:, kk, 256:257], var[:, q1slot, 2 * h + kk, :],
                        start=(kk == 0), stop=False)
        wpt = ev_marks_h + [self.vaug_ones, marks["vaug"]] + (w0 if first else [])
        self.MM(SA, s["a_vaug"][:, h, 0:128], s["a_pt"][:, h, :],
                start=first, stop=True, waits=wpt)
        self.MM(SB, s["a_vaug"][:, h, 128:256], s["a_pt"][:, h, :],
                start=first, stop=True)
        t_S = self.MM(SZ, s["a_vaug"][:, h, 256:257], s["a_pt"][:, h, :],
                      start=first, stop=True, inc_group=True)
        self.var_cons[q1slot] = t_S
        self._head_ts = t_S
        m_z = self.V("tensor_copy", rz_ap, SZ,
                     waits=[t_S, self._rz_cons.get(h % 2, ("t", 0))])
        t_bc = self.MM(ps[bbc][:, 0:128], s["c_ones32_sb"][0:1, :], rz_ap,
                       start=True, stop=True, waits=[m_z] + self.bank_w(bbc),
                       inc_group=True)
        self._rz_cons[h % 2] = t_bc
        m_rzbc = self.V("reciprocal", rzbc_ap, ps[bbc][:, 0:128], waits=[t_bc])
        self.bank_done(bbc, m_rzbc)
        return (b0, rzbc_ap, t_S, m_rzbc)

    def _head_p2(self, c, h, state, marks, m_ku):
        p, s, ps = self.p, self.s, self.ps
        first = (c == 0)
        b0, rzbc_ap, t_S, m_rzbc = state
        SA = ps[b0][:, 0:128]
        SB = ps[b0][:, 128:256]
        m_att = None
        for m in range(2):
            src = SA if m == 0 else SB
            jj = 2 * h + m
            self.V("tensor_mul", s["scr8"][:, 0, 0:L], src, s["a_g"][:, jj, :],
                   waits=[m_rzbc, marks["g"]])
            m_att = self.V("tensor_mul", s["a_att"][:, jj, :], s["scr8"][:, 0, 0:L],
                           rzbc_ap)
        self.bank_done(b0, m_att)

        for kk in range(2):
            t_tr = self.TR(self.ps[7].bitcast(BF16)[:, kk * 128:(kk + 1) * 128],
                           s["a_ku"][:, 2 * h + kk, :], s["c_id16_sb"][:],
                           waits=([m_ku] + self.bank_w(7)) if kk == 0 else (),
                           inc_group=(kk == 1))
        self.V("tensor_copy", s["a_kutm"][:, 0, :], self.ps[7].bitcast(BF16)[:, 0:128],
               waits=[t_tr])
        m_kut = self.V("tensor_copy", s["a_kutm"][:, 1, :],
                       self.ps[7].bitcast(BF16)[:, 128:256])
        self.bank_done(7, m_kut)
        for m in range(2):
            bank = 3 if m == 0 else 4
            t_A = self.MM(ps[bank][:, 0:257], s["a_kutm"][:, m, :], s["a_vaug"][:, h, :],
                          start=True, stop=True,
                          waits=([m_kut, marks["vaug"]] + self.bank_w(3) + self.bank_w(4))
                          if m == 0 else (),
                          inc_group=(m == 1))
        m_A = None
        for m in range(2):
            bank = 3 if m == 0 else 4
            wL = s["t_wL"][:, 2 * h + m:2 * h + m + 1]
            if first:
                m_A = self.V("tensor_copy", s["a_A"][:, 2 * h + m, :],
                             ps[bank][:, 0:257], waits=[t_A] if m == 0 else ())
            else:
                m_A = self.V("scalar_tensor_tensor", s["a_A"][:, 2 * h + m, :],
                             s["a_A"][:, 2 * h + m, :], wL, ps[bank][:, 0:257],
                             ALU.mult, ALU.add, waits=[t_A] if m == 0 else ())
        self.bank_done(3, m_A)
        self.bank_done(4, m_A)
        self.Aup = m_A
        self.vaug_cons = t_A
        self.ku_cons = t_tr
        return m_att, t_S

    def phase_mark(self):
        p = self.p
        return [("t", p.now("t")), ("v", p.now("v")), ("s", p.now("s")),
                ("y", p.now("y")), ("g", p.now("g"))]

    def load_expert(self, e, slot_cons):
        p, d, s = self.p, self.d, self.s
        dst = s["w_ffn0"] if e % 2 == 0 else s["w_ffn1"]
        p.rec("gpsimd", "dma_start",
              out=dst[:, 0:8192].rearrange("p (two j o) -> p two j o", two=2, j=NCH),
              in_=d["ffn_w"][e, 0:2 * C * I_FFN].rearrange(
                  "(two j p o) -> p two j o", two=2, j=NCH, p=128),
              waits=slot_cons, inc="g", incn=16)
        p.rec("gpsimd", "dma_start",
              out=dst[:, 8192:12288].rearrange("p (j o) -> p j o", j=4),
              in_=d["ffn_w"][e, 2 * C * I_FFN:].rearrange("(j p o) -> p j o", j=4, p=128),
              inc="g", incn=16)
        return ("g", p.now("g"))

    def _ln_chunk_to(self, dst_ap, src_ap, waits, t0):
        p, s, ps = self.p, self.s, self.ps
        ones32 = s["c_ones32_sb"]
        m_xsq = self.V("tensor_mul", s["scr8"][:, :, 0:L], src_ap, src_ap,
                       waits=waits)
        for j in range(NCH):
            self.MM(ps[2][:, 0:L], ones32[:], s["resid"][:, j, t0:t0 + L],
                    start=(j == 0), stop=(j == NCH - 1),
                    waits=([m_xsq] + self.bank_w(2)) if j == 0 else ())
        for j in range(NCH):
            t_ln = self.MM(ps[2][:, 136:136 + L], ones32[:], s["scr8"][:, j, 0:L],
                           start=(j == 0), stop=(j == NCH - 1),
                           waits=[m_xsq] if j == 0 else (), inc_group=(j == NCH - 1))
        self.V("tensor_scalar_mul", s["ln_m"][:, 0:L], ps[2][:, 0:L], 1.0 / C,
               waits=[t_ln])
        self.V("tensor_mul", s["ln_r"][:, 0:L], s["ln_m"][:, 0:L], s["ln_m"][:, 0:L])
        self.V("scalar_tensor_tensor", s["ln_r"][:, 0:L], ps[2][:, 136:136 + L],
               1.0 / C, s["ln_r"][:, 0:L], ALU.mult, ALU.subtract)
        m_var = self.V("tensor_scalar_add", s["ln_r"][:, 0:L], s["ln_r"][:, 0:L], 1e-5)
        self.bank_done(2, m_var)
        self.S("activation", s["ln_r"][:, 0:L], s["ln_r"][:, 0:L], AF.Ln, waits=[m_var])
        m_rstd = self.S("activation", s["ln_r"][:, 0:L], s["ln_r"][:, 0:L], AF.Exp,
                        scale=-0.5)
        mb = s["ln_m"][:, 0:L].unsqueeze(1).broadcast_to([128, NCH, L])
        rb = s["ln_r"][:, 0:L].unsqueeze(1).broadcast_to([128, NCH, L])
        self.V("tensor_sub", s["scr8"][:, :, 0:L], src_ap, mb, waits=[m_rstd])
        return self.V("tensor_mul", dst_ap, s["scr8"][:, :, 0:L], rb)

    def gen_ffn(self, batch, call):
        p, s, ps, d = self.p, self.s, self.ps, self.d
        pm = self.phase_mark()
        p.rec("gpsimd", "dma_start", out=s["w_gate"][:, :, :],
              in_=d["gate_wT"][:, :].rearrange("(j p) o -> p j o", p=128),
              waits=pm, inc="g", incn=16)
        p.rec("sync", "dma_start", out=s["sel16_sb"][:, :], in_=d["c_sel"][:, :],
              waits=pm, inc="y", incn=16)
        gatew = [("g", p.now("g")), ("y", p.now("y"))]
        slot_use = [pm, pm]
        eld = {}
        eld[0] = self.load_expert(0, pm)
        eld[1] = self.load_expert(1, pm)

        m_lgtr = None
        for c in range(NCHUNK):
            t0 = c * L
            m_xln = self._ln_chunk_to(s["x_ln"][:, :, t0:t0 + L],
                                      s["resid"][:, :, t0:t0 + L],
                                      pm if c == 0 else (), t0)
            for j in range(NCH):
                t_lg = self.MM(ps[0][0:8, 0:L], s["w_gate"][:, j, :],
                               s["x_ln"][:, j, t0:t0 + L],
                               start=(j == 0), stop=(j == NCH - 1),
                               waits=([m_xln] + gatew + self.bank_w(0)) if j == 0 else (),
                               inc_group=(j == NCH - 1))
            m_lg = self.V("tensor_copy", s["lg_sb"][:, :], ps[0][0:8, 0:L], waits=[t_lg])
            self.bank_done(0, m_lg)
            t_tr = self.TR(ps[1][:, 0:8], s["lg_sb"][:, :], s["c_id32_sb"][0:8, 0:8],
                           waits=[m_lg] + self.bank_w(1), inc_group=True)
            m_lgtr = self.V("tensor_copy", s["lgT"][:, c, :], ps[1][:, 0:8], waits=[t_tr])
            self.bank_done(1, m_lgtr)
            self.xln_mark = m_xln

        def SPC():
            self.V("tensor_copy", s["scr8"][:, 7, 0:128], s["c_ones32_sb"][:, :])

        lgw = s["lgT"][:, :, :]
        self.V("tensor_copy", s["scr8"][:, 7, 0:128], s["c_ones32_sb"][:, :],
               waits=[m_lgtr])
        m_lgtr = self.V("tensor_copy", s["scr8"][:, 7, 0:128], s["c_ones32_sb"][:, :])
        self.S("activation", s["gw_sc"][:, :, :], lgw, AF.Tanh, scale=0.5,
               waits=[m_lgtr])
        self.V("tensor_scalar", s["gw_sc"][:, :, :], s["gw_sc"][:, :, :], 0.5, 0.5,
               ALU.mult, ALU.add, waits=[("s", p.now("s"))])
        self.tap(f"fsc_b{batch}f{call}", s["gw_sc"][:, :, :], [("v", p.now("v"))])
        SPC()
        ebias3 = s["e_bias_sb"][:, :].unsqueeze(1).broadcast_to([128, NCHUNK, E_FFN])
        eidx3 = s["eidx_sb"][:, :].unsqueeze(1).broadcast_to([128, NCHUNK, E_FFN])
        self.V("tensor_tensor", s["gw_tmp"][:, :, :], s["gw_sc"][:, :, :], ebias3,
               ALU.add)
        self.tap(f"ftmp_b{batch}f{call}", s["gw_tmp"][:, :, :], [("v", p.now("v"))])
        SPC()
        self.V("tensor_tensor", s["gw_gs"][:, :, :], s["gw_tmp"][:, :, 0:8:2],
               s["gw_tmp"][:, :, 1:8:2], ALU.add)
        self.tap(f"fgs_b{batch}f{call}", s["gw_gs"][:, :, :], [("v", p.now("v"))])
        SPC()
        self.V("tensor_max", s["gw_h2"][:, :, :], s["gw_gs"][:, :, 0:2],
               s["gw_gs"][:, :, 2:4])
        SPC()
        self.V("tensor_max", s["gw_m1"][:, :, :], s["gw_h2"][:, :, 0:1],
               s["gw_h2"][:, :, 1:2])
        SPC()
        self.V("tensor_tensor", s["gw_t1"][:, :, :], s["gw_gs"][:, :, :],
               s["gw_m1"][:, :, :].broadcast_to([128, NCHUNK, 4]), ALU.is_ge)
        SPC()
        self.V("scalar_tensor_tensor", s["gw_g2"][:, :, :], s["gw_t1"][:, :, :], -1e9,
               s["gw_gs"][:, :, :], ALU.mult, ALU.add)
        SPC()
        self.V("tensor_max", s["gw_h2"][:, :, :], s["gw_g2"][:, :, 0:2],
               s["gw_g2"][:, :, 2:4])
        SPC()
        self.V("tensor_max", s["gw_m2"][:, :, :], s["gw_h2"][:, :, 0:1],
               s["gw_h2"][:, :, 1:2])
        SPC()
        self.V("tensor_tensor", s["gw_t2"][:, :, :], s["gw_g2"][:, :, :],
               s["gw_m2"][:, :, :].broadcast_to([128, NCHUNK, 4]), ALU.is_ge)
        SPC()
        self.V("tensor_add", s["gw_t1"][:, :, :], s["gw_t1"][:, :, :],
               s["gw_t2"][:, :, :])
        self.tap(f"gmask_b{batch}f{call}", s["gw_t1"][:, :, :], [("v", p.now("v"))])
        SPC()
        self.V("tensor_copy", s["gw_sm"][:, :, 0:8:2], s["gw_t1"][:, :, :])
        self.V("tensor_copy", s["gw_sm"][:, :, 1:8:2], s["gw_t1"][:, :, :])
        SPC()
        self.V("tensor_mul", s["gw_msk"][:, :, :], s["gw_tmp"][:, :, :],
               s["gw_sm"][:, :, :])
        SPC()
        self.V("tensor_scalar", s["gw_z"][:, :, :], s["gw_msk"][:, :, :], 0.0, None,
               ALU.is_equal)
        SPC()
        self.V("tensor_mul", s["gw_z"][:, :, :], s["gw_z"][:, :, :], eidx3)
        SPC()
        self.V("tensor_sub", s["gw_key"][:, :, :], s["gw_msk"][:, :, :],
               s["gw_z"][:, :, :])
        SPC()
        self.V("tensor_scalar", s["gw_sm"][:, :, :], s["gw_msk"][:, :, :], 0.0, None,
               ALU.is_lt)
        SPC()
        self.V("scalar_tensor_tensor", s["gw_key"][:, :, :], s["gw_sm"][:, :, :], -8.0,
               s["gw_key"][:, :, :], ALU.mult, ALU.add)
        SPC()
        self.V("tensor_max", s["gw_gs"][:, :, :], s["gw_key"][:, :, 0:4],
               s["gw_key"][:, :, 4:8])
        SPC()
        self.V("tensor_max", s["gw_h2"][:, :, :], s["gw_gs"][:, :, 0:2],
               s["gw_gs"][:, :, 2:4])
        SPC()
        self.V("tensor_max", s["gw_m1"][:, :, :], s["gw_h2"][:, :, 0:1],
               s["gw_h2"][:, :, 1:2])
        SPC()
        self.V("tensor_tensor", s["gw_sm"][:, :, :], s["gw_key"][:, :, :],
               s["gw_m1"][:, :, :].broadcast_to([128, NCHUNK, E_FFN]), ALU.is_ge)
        SPC()
        self.V("scalar_tensor_tensor", s["gw_z"][:, :, :], s["gw_sm"][:, :, :], -1e9,
               s["gw_key"][:, :, :], ALU.mult, ALU.add)
        SPC()
        self.V("tensor_max", s["gw_t1"][:, :, :], s["gw_z"][:, :, 0:4],
               s["gw_z"][:, :, 4:8])
        SPC()
        self.V("tensor_max", s["gw_h2"][:, :, :], s["gw_t1"][:, :, 0:2],
               s["gw_t1"][:, :, 2:4])
        SPC()
        self.V("tensor_max", s["gw_m2"][:, :, :], s["gw_h2"][:, :, 0:1],
               s["gw_h2"][:, :, 1:2])
        SPC()
        self.V("tensor_tensor", s["gw_tmp"][:, :, :], s["gw_z"][:, :, :],
               s["gw_m2"][:, :, :].broadcast_to([128, NCHUNK, E_FFN]), ALU.is_ge)
        SPC()
        self.V("tensor_add", s["gw_sm"][:, :, :], s["gw_sm"][:, :, :],
               s["gw_tmp"][:, :, :])
        self.tap(f"fsel_b{batch}f{call}", s["gw_sm"][:, :, :], [("v", p.now("v"))])
        SPC()
        self.V("tensor_mul", s["gw_msk"][:, :, :], s["gw_sm"][:, :, :],
               s["gw_sc"][:, :, :])
        SPC()
        self.V("tensor_add", s["gw_gs"][:, :, :], s["gw_msk"][:, :, 0:4],
               s["gw_msk"][:, :, 4:8])
        SPC()
        self.V("tensor_add", s["gw_h2"][:, :, :], s["gw_gs"][:, :, 0:2],
               s["gw_gs"][:, :, 2:4])
        SPC()
        self.V("tensor_add", s["gw_m1"][:, :, :], s["gw_h2"][:, :, 0:1],
               s["gw_h2"][:, :, 1:2])
        SPC()
        self.V("tensor_scalar_add", s["gw_se"][:, :, :], s["gw_m1"][:, :, :], 1e-20)
        SPC()
        self.V("reciprocal", s["gw_rs"][:, :, :], s["gw_se"][:, :, :])
        SPC()
        m_cb = self.V("tensor_tensor", s["cb_tm"][:, :, :], s["gw_msk"][:, :, :],
                      s["gw_rs"][:, :, :].broadcast_to([128, NCHUNK, E_FFN]), ALU.mult)
        cb_mark = m_cb
        self.tap(f"cbtm_b{batch}f{call}", s["cb_tm"][:, :, :], [cb_mark])
        self.tap(f"lgT_b{batch}f{call}", s["lgT"][:, :, :], [cb_mark])
        self.tap(f"xlnf_b{batch}f{call}", s["x_ln"][:, :, 0:256], [cb_mark])
        self.tap(f"xlna_b{batch}f{call}", s["x_ln"][:, :, :], [cb_mark])

        MACRO = 1024
        TILE = 512
        fy = s["f_y"]
        m_res = None
        for mi in range(T // MACRO):
            mt0 = mi * MACRO

            def emit_staging(mi=mi, mt0=mt0):
                m_cfm = None
                for cc in range(mt0 // L, (mt0 + MACRO) // L):
                    off = (cc - mt0 // L) * 128
                    self.V("tensor_copy", s["gw_stg"][:, :],
                           s["cb_tm"][:, cc, :],
                           waits=[cb_mark, getattr(self, "cbfm_cons", ("t", 0))])
                    m_c16 = self.V("tensor_copy", s["scr8"][:, 7, 0:128],
                                   s["c_ones32_sb"][:, :])
                    t_ctr = self.TR(ps[5].bitcast(BF16)[0:8, 0:128],
                                    s["gw_stg"][:, :], s["c_id16_sb"][:],
                                    waits=[m_c16] + self.bank_w(5), inc_group=True)
                    m_cfm = self.V("tensor_copy", s["cb_fm16"][0:8, off:off + 128],
                                   ps[5].bitcast(BF16)[0:8, 0:128], waits=[t_ctr])
                    self.bank_done(5, m_cfm)
                return m_cfm

            for e in range(E_FFN):
                slot = s["w_ffn0"] if e % 2 == 0 else s["w_ffn1"]
                for ti in range(MACRO // TILE):
                    t0 = mt0 + ti * TILE
                    for m in range(4):
                        for kk in range(NCH):
                            t_g = self.MM(ps[m][:, 0:TILE],
                                          slot[:, kk * 512 + m * 128:kk * 512 + (m + 1) * 128],
                                          s["x_ln"][:, kk, t0:t0 + TILE],
                                          start=(kk == 0), stop=(kk == NCH - 1),
                                          waits=([eld[e], self.xln_mark]
                                                 + self.bank_w(0) + self.bank_w(1)
                                                 + self.bank_w(2) + self.bank_w(3))
                                          if (m == 0 and kk == 0) else (),
                                          inc_group=(kk == NCH - 1))
                        m_sg = self.S("activation", s["f_sg"][:, m, :],
                                      ps[m][:, 0:TILE], AF.Silu, waits=[t_g])
                        self.bank_done(m, m_sg)
                    if e == 0 and ti == 0:
                        m_cfm = emit_staging()
                    t_bc = self.MM(ps[4][:, 0:TILE],
                                   s["sel16_sb"][:, e * 128:(e + 1) * 128],
                                   s["cb_fm16"][0:8, ti * TILE:(ti + 1) * TILE],
                                   start=True, stop=True,
                                   waits=[m_cfm] + self.bank_w(4), inc_group=True)
                    m_cbb = self.V("tensor_copy", s["cb_bc"][:, ti % 2, :],
                                   ps[4][:, 0:TILE],
                                   waits=[t_bc, getattr(self, "cbb_cons", ("v", 0))])
                    self.bank_done(4, m_cbb)
                    self.cbfm_cons = t_bc
                    m_h2 = None
                    for m in range(4):
                        for kk in range(NCH):
                            t_u = self.MM(ps[m][:, 0:TILE],
                                          slot[:, 4096 + kk * 512 + m * 128:4096 + kk * 512 + (m + 1) * 128],
                                          s["x_ln"][:, kk, t0:t0 + TILE],
                                          start=(kk == 0), stop=(kk == NCH - 1),
                                          waits=self.bank_w(m) if kk == 0 else (),
                                          inc_group=(kk == NCH - 1))
                        self.V("tensor_tensor", s["f_sg"][:, m, :],
                               s["f_sg"][:, m, :], ps[m][:, 0:TILE],
                               ALU.mult, waits=[t_u])
                        m_h = self.V("tensor_tensor", s["f_sg"][:, m, :],
                                     s["f_sg"][:, m, :],
                                     s["cb_bc"][:, ti % 2, :], ALU.mult,
                                     waits=[m_cbb] if m == 0 else ())
                        self.bank_done(m, m_h)
                        m_h2 = m_h
                    m_y = None
                    for m in range(8):
                        bank = 4 + (m % 4)
                        for ji in range(4):
                            t_d = self.MM(ps[bank][:, 0:TILE],
                                          slot[:, 8192 + ji * 1024 + m * 128:8192 + ji * 1024 + (m + 1) * 128],
                                          s["f_sg"][:, ji, :],
                                          start=(ji == 0), stop=(ji == 3),
                                          waits=([m_h2] + self.bank_w(bank))
                                          if ji == 0 else (),
                                          inc_group=(ji == 3))
                        dst = fy[:, m, (t0 - mt0):(t0 - mt0) + TILE]
                        if e == 0:
                            m_y = self.V("tensor_copy", dst, ps[bank][:, 0:TILE],
                                         waits=[t_d])
                        else:
                            m_y = self.V("tensor_add", dst, dst, ps[bank][:, 0:TILE],
                                         waits=[t_d])
                        self.bank_done(bank, m_y)
                    self.cbb_cons = m_h2
                    slot_use[e % 2] = t_d
                if mi == 0:
                    self.tap(f"fy_e{e}_b{batch}f{call}", s["f_y"][:, :, :], [m_y])
                pos = mi * E_FFN + e
                nxt = pos + 2
                if nxt < (T // MACRO) * E_FFN:
                    e2 = nxt % E_FFN
                    eld[e2] = self.load_expert(e2, [slot_use[e % 2]])
            for jj in range(4):
                m_res = self.V("tensor_add",
                               s["resid"][:, 2 * jj:2 * jj + 2, mt0:mt0 + MACRO],
                               s["resid"][:, 2 * jj:2 * jj + 2, mt0:mt0 + MACRO],
                               fy[:, 2 * jj:2 * jj + 2, :],
                               waits=[m_y] if jj == 0 else ())
        self.attA = m_res
        self.ffn_done = self.phase_mark()


    def gen_attn_b(self, batch):
        p, s, ps, d = self.p, self.s, self.ps, self.d
        pm = self.phase_mark()
        TQ = 512
        NT = T // TQ
        if not hasattr(self, "exps_cons"):
            self.exps_cons = {}

        p.rec("gpsimd", "dma_start", out=s["wb_o"][:, :, :],
              in_=d["wo_b"][:, :].rearrange("(j p) o -> p j o", p=128),
              waits=pm, inc="g", incn=16)
        m_wo = ("g", p.now("g"))

        def load_qkv(hg, waits):
            c0 = hg * 256
            for i, nm in enumerate(("wb_q", "wb_k", "wb_v")):
                p.rec("gpsimd", "dma_start", out=s[nm][:, :, :],
                      in_=d["wqkvT_b"][i, :, c0:c0 + 256].rearrange(
                          "(j p) o -> p j o", p=128),
                      waits=waits if i == 0 else (), inc="g", incn=16)
            return ("g", p.now("g"))

        m_qkvw = load_qkv(0, pm)

        m_xln = None
        for c in range(NCHUNK):
            t0 = c * L
            m_xln = self._ln_chunk_to(s["x_ln"][:, :, t0:t0 + L],
                                      s["resid"][:, :, t0:t0 + L],
                                      pm if c == 0 else (), t0)

        m_shift = ("y", p.now("y"))
        last_att = ("v", p.now("v"))
        for hg in range(4):
            m_qk = None
            for wname, dstname, bcol in (("wb_q", "qf_b", 0), ("wb_k", "kf_b", 1)):
                for mj in range(2):
                    for tt in range(NT):
                        t0 = tt * TQ
                        bank = (tt + mj * NT) % 2
                        for j in range(NCH):
                            t_p = self.MM(ps[bank][:, 0:TQ],
                                          s[wname][:, j, mj * 128:(mj + 1) * 128],
                                          s["x_ln"][:, j, t0:t0 + TQ],
                                          start=(j == 0), stop=(j == NCH - 1),
                                          waits=([m_qkvw, m_xln]
                                                 + self.bank_w(bank))
                                          if j == 0 else (),
                                          inc_group=(j == NCH - 1))
                        m_qk = self.V("tensor_scalar",
                                      s[dstname][:, mj, t0:t0 + TQ],
                                      ps[bank][:, 0:TQ],
                                      s["bqk_sb"][:, 2 * hg + mj, bcol:bcol + 1],
                                      None, ALU.add, waits=[t_p])
                        self.bank_done(bank, m_qk)
            if hg == 0:
                self.tap(f"bqf_b{batch}", s["qf_b"][:, :, :], [m_qk])
                self.tap(f"bkf_b{batch}", s["kf_b"][:, :, :], [m_qk])
            m_vt = None
            for tc in range(NCHUNK):
                t0 = tc * L
                bank = 2 + tc % 2
                for j in range(NCH):
                    t_v = self.MM(ps[bank][:, 0:256],
                                  s["x_ln"][:, j, t0:t0 + L],
                                  s["wb_v"][:, j, 0:256],
                                  start=(j == 0), stop=(j == NCH - 1),
                                  waits=([m_qkvw, m_xln] + self.bank_w(bank))
                                  if j == 0 else (),
                                  inc_group=(j == NCH - 1))
                for hh in range(4):
                    self.V("tensor_tensor",
                           s["vt_b"][:, tc, hh * 65:hh * 65 + 64],
                           ps[bank][:, hh * 64:(hh + 1) * 64],
                           s["bv_bc_sb"][:, hg * 256 + hh * 64:
                                         hg * 256 + (hh + 1) * 64],
                           ALU.add, waits=[t_v] if hh == 0 else ())
                m_vt = self.V("tensor_copy", s["vt_b"][:, tc, 64:260:65],
                              s["c_ones32_sb"][:, 0:4])
                self.bank_done(bank, m_vt)
            if hg == 0:
                self.tap(f"bvt_b{batch}", s["vt_b"][:, :, :], [m_vt])
            if hg < 3:
                nxt_qkvw = load_qkv(hg + 1, [m_qk, m_vt])

            for mj in range(2):
                slot = 2 * hg + mj
                for tt in range(NT):
                    tq0 = tt * TQ
                    tp = (mj * NT + tt) % 2
                    avA = 4 if tp == 0 else 6
                    avB = 5 if tp == 0 else 7
                    t_avA = t_avB = None

                    def emit_scores(tkc, first):
                        tk0 = tkc * L
                        bA = tkc % 2
                        bB = 2 + tkc % 2
                        wA = self.bank_w(bA)
                        if first:
                            wA = wA + [m_qk, m_vt]
                        t_sA = self.MM(ps[bA][:, 0:TQ],
                                       s["kf_b"][0:64, mj, tk0:tk0 + L],
                                       s["qf_b"][0:64, mj, tq0:tq0 + TQ],
                                       start=True, stop=True,
                                       tile_position=(0, 0),
                                       waits=wA, inc_group=True)
                        t_sB = self.MM(ps[bB][:, 0:TQ],
                                       s["kf_b"][64:128, mj, tk0:tk0 + L],
                                       s["qf_b"][64:128, mj, tq0:tq0 + TQ],
                                       start=True, stop=True,
                                       tile_position=(64, 0),
                                       waits=self.bank_w(bB), inc_group=True)
                        return t_sA, t_sB

                    pend = [emit_scores(0, tt == 0 and mj == 0),
                            emit_scores(1, False)]
                    for tkc in range(NCHUNK):
                        bA = tkc % 2
                        bB = 2 + tkc % 2
                        t_sA, t_sB = pend[tkc % 2]
                        eA = s["exps"][:, (tkc % 2) * 4:(tkc % 2) * 4 + 2, :]
                        eB = s["exps"][:, (tkc % 2) * 4 + 2:(tkc % 2) * 4 + 4, :]
                        m_eA = self.S("activation", eA,
                                      ps[bA][:, 0:TQ].rearrange(
                                          "p (a b) -> p a b", b=256),
                                      AF.Exp, bias=s["c_n30_sb"][:, 0:1],
                                      scale=0.125,
                                      waits=[t_sA, self.exps_cons.get(
                                          (tkc % 2, 0), ("t", 0))])
                        m_eB = self.S("activation", eB,
                                      ps[bB][:, 0:TQ].rearrange(
                                          "p (a b) -> p a b", b=256),
                                      AF.Exp, bias=s["c_n30_sb"][:, 0:1],
                                      scale=0.125,
                                      waits=[t_sB, self.exps_cons.get(
                                          (tkc % 2, 1), ("t", 0))])
                        self.bank_done(bA, m_eA)
                        self.bank_done(bB, m_eB)
                        if tkc + 2 < NCHUNK:
                            pend[tkc % 2] = emit_scores(tkc + 2, False)
                        t_avA = self.MM(ps[avA][0:65, 0:TQ],
                                        s["vt_b"][:, tkc,
                                                  (2 * mj) * 65:(2 * mj) * 65 + 65],
                                        eA.rearrange("p a b -> p (a b)"),
                                        start=(tkc == 0),
                                        stop=(tkc == NCHUNK - 1),
                                        waits=([m_eA] + self.bank_w(avA))
                                        if tkc == 0 else [m_eA],
                                        inc_group=(tkc == NCHUNK - 1))
                        t_avB = self.MM(ps[avB][0:65, 0:TQ],
                                        s["vt_b"][:, tkc,
                                                  (2 * mj + 1) * 65:
                                                  (2 * mj + 1) * 65 + 65],
                                        eB.rearrange("p a b -> p (a b)"),
                                        start=(tkc == 0),
                                        stop=(tkc == NCHUNK - 1),
                                        waits=([m_eB] + self.bank_w(avB))
                                        if tkc == 0 else [m_eB],
                                        inc_group=(tkc == NCHUNK - 1))
                        self.exps_cons[(tkc % 2, 0)] = t_avA
                        self.exps_cons[(tkc % 2, 1)] = t_avB
                    m_zA = self.V("tensor_copy", s["b_rz"][64:65, 0:TQ],
                                  ps[avA][64:65, 0:TQ], waits=[t_avA])
                    t_bcA = self.MM(ps[0][0:64, 0:TQ],
                                    s["c_ones32_sb"][64:65, 0:64],
                                    s["b_rz"][64:65, 0:TQ],
                                    start=True, stop=True,
                                    waits=[m_zA] + self.bank_w(0),
                                    inc_group=True)
                    m_rcA = self.V("reciprocal", s["b_rzbc"][0:64, 0:TQ],
                                   ps[0][0:64, 0:TQ], waits=[t_bcA])
                    self.bank_done(0, m_rcA)
                    m_aA = self.V("tensor_mul",
                                  s["att_b"][0:64, slot, tq0:tq0 + TQ],
                                  ps[avA][0:64, 0:TQ], s["b_rzbc"][0:64, 0:TQ])
                    m_zB = self.V("tensor_copy", s["b_rz"][64:65, 0:TQ],
                                  ps[avB][64:65, 0:TQ], waits=[t_avB, t_bcA])
                    t_bcB = self.MM(ps[1][0:64, 0:TQ],
                                    s["c_ones32_sb"][64:65, 0:64],
                                    s["b_rz"][64:65, 0:TQ],
                                    start=True, stop=True,
                                    waits=[m_zB] + self.bank_w(1),
                                    inc_group=True)
                    m_rcB = self.V("reciprocal", s["b_rzbc"][0:64, 0:TQ],
                                   ps[1][0:64, 0:TQ], waits=[t_bcB])
                    self.bank_done(1, m_rcB)
                    m_aB = self.V("tensor_mul", s["att_tmp"][0:64, 0:TQ],
                                  ps[avB][0:64, 0:TQ], s["b_rzbc"][0:64, 0:TQ],
                                  waits=[m_shift])
                    self.bank_done(avA, m_aA)
                    self.bank_done(avB, m_aB)
                    p.rec("sync", "dma_start",
                          out=s["att_b"][64:128, slot, tq0:tq0 + TQ],
                          in_=s["att_tmp"][0:64, 0:TQ],
                          waits=[m_aB], inc="y", incn=16)
                    m_shift = ("y", p.now("y"))
                    last_att = m_aA
            m_qkvw = nxt_qkvw if hg < 3 else m_qkvw

        self.tap(f"batt_b{batch}", s["att_b"][:, :, :], [last_att])
        m_res = None
        for tt in range(NT):
            tq0 = tt * TQ
            for m in range(NCH):
                bank = (m + tt * NCH) % 2
                for j in range(NCH):
                    t_o = self.MM(ps[bank][:, 0:TQ],
                                  s["wb_o"][:, j, m * 128:(m + 1) * 128],
                                  s["att_b"][:, j, tq0:tq0 + TQ],
                                  start=(j == 0), stop=(j == NCH - 1),
                                  waits=([m_wo, last_att, m_shift]
                                         + self.bank_w(bank))
                                  if j == 0 else (),
                                  inc_group=(j == NCH - 1))
                self.V("tensor_scalar", s["b_rz"][:, 0:TQ], ps[bank][:, 0:TQ],
                       s["bo_sb"][:, m, 0:1], None, ALU.add, waits=[t_o])
                m_res = self.V("tensor_add", s["resid"][:, m, tq0:tq0 + TQ],
                               s["resid"][:, m, tq0:tq0 + TQ],
                               s["b_rz"][:, 0:TQ])
                self.bank_done(bank, m_res)
        self.attA = m_res

    def gen_store(self, batch):
        p, s, d = self.p, self.d if False else self.s, self.d
        s = self.s
        p.rec("sync", "dma_start", out=d["out"][batch].rearrange("(j p) t -> p j t", p=128),
              in_=s["resid"][:, :, :],
              waits=[self.attA, ("v", p.now("v"))], inc="y", incn=16)
        self.st = ("y", p.now("y"))

    def finalize(self):
        p = self.p
        p.rec("sync", "_wait_only", waits=[("y", p.now("y")), ("g", p.now("g"))])
        p.emit()
        return self.nc



def _prep_inputs(inputs):
    f32 = np.float32
    x = np.ascontiguousarray(np.asarray(inputs["L0_attn_a_x"], f32).transpose(0, 2, 1))
    per = {}
    per["wqT"] = np.ascontiguousarray(np.asarray(inputs["L0_attn_a_Wq"], f32).T)
    per["wkT"] = np.ascontiguousarray(np.asarray(inputs["L0_attn_a_Wk"], f32).T)
    per["wvT"] = np.ascontiguousarray(np.asarray(inputs["L0_attn_a_Wv"], f32).T)
    per["wgT"] = np.ascontiguousarray(np.asarray(inputs["L0_attn_a_Wg"], f32).T)
    per["woT"] = np.ascontiguousarray(np.asarray(inputs["L0_attn_a_Wo"], f32).T)
    a_small = np.stack([
        np.asarray(inputs["L0_attn_a_time_decay"], f32),
        np.asarray(inputs["L0_attn_a_mix_q"], f32).reshape(-1),
        np.asarray(inputs["L0_attn_a_mix_k"], f32).reshape(-1),
        np.asarray(inputs["L0_attn_a_mix_v"], f32).reshape(-1),
        np.asarray(inputs["L0_attn_a_mix_g"], f32).reshape(-1),
        np.asarray(inputs["L0_attn_a_norm_scale"], f32),
    ], axis=1)
    per["a_small"] = np.ascontiguousarray(a_small)
    gp = np.asarray(inputs["L0_ffn_gate_proj"], f32)
    up = np.asarray(inputs["L0_ffn_up_proj"], f32)
    dn = np.asarray(inputs["L0_ffn_down_proj"], f32)
    ffn_w = np.concatenate([
        gp.transpose(0, 2, 1).reshape(E_FFN, -1),
        up.transpose(0, 2, 1).reshape(E_FFN, -1),
        dn.transpose(0, 2, 1).reshape(E_FFN, -1),
    ], axis=1)
    per["ffn_w"] = np.ascontiguousarray(ffn_w)
    per["gate_wT"] = np.ascontiguousarray(np.asarray(inputs["L0_ffn_gate_weight"], f32).T)
    per["e_bias_bc"] = np.ascontiguousarray(
        np.tile(np.asarray(inputs["L0_ffn_e_score_correction_bias"], f32)[None, :], (128, 1)))
    per["eidx_bc"] = np.ascontiguousarray(
        np.tile(np.arange(E_FFN, dtype=f32)[None, :], (128, 1)))
    wqkv = np.asarray(inputs["L0_attn_b_W_qkv"], f32)
    bqkv = np.asarray(inputs["L0_attn_b_b_qkv"], f32)
    wq = wqkv[:, :, 0:DB].transpose(1, 0, 2).reshape(C, C)
    wk = wqkv[:, :, DB:2 * DB].transpose(1, 0, 2).reshape(C, C)
    wv = wqkv[:, :, 2 * DB:].transpose(1, 0, 2).reshape(C, C)
    per["wqkvT_b"] = np.ascontiguousarray(np.stack([wq, wk, wv]))
    per["wo_b"] = np.ascontiguousarray(np.asarray(inputs["L0_attn_b_W_o"], f32))
    per["bqkT"] = np.ascontiguousarray(np.stack(
        [bqkv[:, 0:DB].reshape(-1), bqkv[:, DB:2 * DB].reshape(-1)], axis=1))
    per["bv_bc"] = np.ascontiguousarray(np.tile(bqkv[:, 2 * DB:].reshape(-1)[None, :], (128, 1)))
    per["bo_b"] = np.ascontiguousarray(np.asarray(inputs["L0_attn_b_b_o"], f32)[:, None])
    return x, per


def _build(n_phases, n_batches, taps=()):
    b = Builder(n_phases, n_batches, taps)
    b.gen_consts()
    b.gen_warmup()
    for batch in range(n_batches):
        b.gen_load_x(batch, waits=[("y", b.p.now("y"))])
        b.gen_tables()
        b.gen_attn_a_weights(waits=b.phase_mark())
        b.gen_attn_a(batch)
        if n_phases >= 2:
            b.gen_ffn(batch, 1)
        if n_phases >= 3:
            b.gen_attn_b(batch)
        if n_phases >= 4:
            b.gen_ffn(batch, 2)
        b.gen_store(batch)
    return b.finalize()


_CACHE = {}


def kernel(**inputs):
    x, per = _prep_inputs(inputs)
    key = (N_PHASES, NB)
    if key not in _CACHE:
        _CACHE[key] = _build(*key)
    nc = _CACHE[key]
    in_maps = []
    for core in range(8):
        m = dict(per)
        m["x"] = np.ascontiguousarray(x[2 * core:2 * core + 2])
        in_maps.append(m)
    res = run_bass_kernel_spmd(nc, in_maps, core_ids=list(range(8)))
    outs = []
    for core in range(8):
        o = np.asarray(res.results[core]["out"], np.float32)
        outs.append(o.transpose(0, 2, 1))
    return np.concatenate(outs, axis=0)


# version 11
# speedup vs baseline: 1.0743x; 1.0743x over previous
import numpy as np
import ml_dtypes
import concourse.bass as bass
import concourse.mybir as mybir
from concourse.bass_utils import run_bass_kernel_spmd

F32 = mybir.dt.float32
BF16 = mybir.dt.bfloat16
U8 = mybir.dt.uint8
AF = mybir.ActivationFunctionType
ALU = mybir.AluOpType
BFNP = ml_dtypes.bfloat16

T = 2048
C = 1024
NCH = 8
L = 128
NCHUNK = T // L
HA = 4
I_FFN, E_FFN = 512, 8
DB = 64
NB = 2

PHASES = ("attn_a", "ffn1", "attn_b", "ffn2")
N_PHASES = 4
SERIAL = False


class Plan:
    def __init__(self, nc):
        self.nc = nc
        self.ops = []
        self.sems = {}
        self.count = {}

    def sem(self, name):
        if name not in self.sems:
            self.sems[name] = self.nc.alloc_semaphore(name)
            self.count[name] = 0
        return self.sems[name]

    def rec(self, engine, method, *args, waits=(), inc=None, incn=1, **kwargs):
        if SERIAL and method != "_wait_only":
            waits = list(waits) + [(n, self.count.get(n, 0))
                                   for n in ("t", "v", "s", "y", "g")]
        incs = []
        if inc is not None:
            self.sem(inc)
            self.count[inc] += incn
            incs.append((inc, incn))
        for w, _ in waits:
            self.sem(w)
        self.ops.append((engine, method, args, kwargs,
                         [(w, v) for (w, v) in waits if v > 0], incs))

    def now(self, name):
        return self.count.get(name, 0)

    def emit(self):
        nc, plan = self.nc, self

        def make(engine_name):
            def run(eng):
                wm = {}
                for (e, method, args, kwargs, waits, incs) in plan.ops:
                    if e != engine_name:
                        continue
                    for (w, v) in waits:
                        if wm.get(w, -1) < v:
                            eng.wait_ge(plan.sems[w], v)
                            wm[w] = v
                    if method == "_wait_only":
                        continue
                    inst = getattr(eng, method)(*args, **kwargs)
                    for (sname, n) in incs:
                        inst.then_inc(plan.sems[sname], n)
            return run

        with nc.Block() as blk:
            blk.tensor(make("tensor"))
            blk.vector(make("vector"))
            blk.scalar(make("scalar"))
            blk.sync(make("sync"))
            blk.gpsimd(make("gpsimd"))


def host_constants():
    tau = np.arange(L)
    masks = []
    levels = [32, 8, 2]
    for li, lsz in enumerate(levels):
        outer = L if li == 0 else levels[li - 1]
        for delta in (1, 2, 3):
            blk = tau // lsz
            valid = (blk[None, :] - blk[:, None] == delta) & \
                    (tau[None, :] // outer == tau[:, None] // outer)
            masks.append(valid.astype(np.uint8))
    valid = (tau[None, :] // 2 == tau[:, None] // 2) & (tau[None, :] >= tau[:, None])
    masks.append(valid.astype(np.uint8))
    maskcat = np.stack(masks).astype(np.uint8)

    pats = [np.arange(130, dtype=np.float64),
            np.asarray([0.0, -1.0])]
    flat = np.concatenate([np.asarray(q, np.float32) for q in pats])
    pat_tile = np.tile(flat[None, :], (128, 1)).astype(np.float32)
    offs, o = [], 0
    for q in pats:
        offs.append(o)
        o += len(q)
    return dict(masks=maskcat, pat=pat_tile, pat_offs=offs)


def dtsize(dt):
    return mybir.dt.size(dt)


class Builder:
    def __init__(self, n_phases=4, n_batches=NB, taps=()):
        self.nc = bass.Bass("TRN2", target_bir_lowering=False)
        self.p = Plan(self.nc)
        self.hc = host_constants()
        self.n_phases = n_phases
        self.n_batches = n_batches
        self.taps = set(taps)
        self.tap_t = {}
        self.bankmark = {}
        self._dram()
        self._sbuf()
        self._psum()

    def tap(self, name, ap, waits):
        if name not in self.taps:
            return
        shape = list(ap.shape)
        t = self.nc.dram_tensor("tap_" + name, shape, ap.dtype, kind="ExternalOutput").ap()
        self.tap_t[name] = t
        self.p.rec("sync", "dma_start", out=t[tuple(slice(None) for _ in shape)],
                   in_=ap, waits=waits, inc="y", incn=16,
                   allow_slow_non_contiguous=True)
        blk = [("y", self.p.now("y"))]
        self.p.rec("vector", "_wait_only", waits=blk)
        self.p.rec("scalar", "_wait_only", waits=blk)
        self.p.rec("tensor", "_wait_only", waits=blk)
        self.p.rec("gpsimd", "_wait_only", waits=blk)

    def _dram(self):
        nc = self.nc
        d = {}
        d["x"] = nc.dram_tensor("x", [NB, C, T], F32, kind="ExternalInput").ap()
        d["out"] = nc.dram_tensor("out", [NB, C, T], F32, kind="ExternalOutput").ap()
        for nm in ("wqT", "wkT", "wvT", "wgT", "woT"):
            d[nm] = nc.dram_tensor(nm, [C, C], F32, kind="ExternalInput").ap()
        d["a_small"] = nc.dram_tensor("a_small", [C, 6], F32, kind="ExternalInput").ap()
        d["ffn_w"] = nc.dram_tensor("ffn_w", [E_FFN, 3 * C * I_FFN], F32, kind="ExternalInput").ap()
        d["gate_wT"] = nc.dram_tensor("gate_wT", [C, E_FFN], F32, kind="ExternalInput").ap()
        d["e_bias_bc"] = nc.dram_tensor("e_bias_bc", [128, E_FFN], F32, kind="ExternalInput").ap()
        d["eidx_bc"] = nc.dram_tensor("eidx_bc", [128, E_FFN], F32, kind="ExternalInput").ap()
        d["bqkT"] = nc.dram_tensor("bqkT", [C, 2], F32, kind="ExternalInput").ap()
        d["wqkvT_b"] = nc.dram_tensor("wqkvT_b", [3, C, C], F32, kind="ExternalInput").ap()
        d["wo_b"] = nc.dram_tensor("wo_b", [C, C], F32, kind="ExternalInput").ap()
        d["bv_bc"] = nc.dram_tensor("bv_bc", [128, C], F32, kind="ExternalInput").ap()
        d["bo_b"] = nc.dram_tensor("bo_b", [C, 1], F32, kind="ExternalInput").ap()
        d["c_masks"] = nc.inline_tensor(self.hc["masks"].reshape(10, 128 * 128), name="c_masks").ap()
        d["c_pat"] = nc.inline_tensor(self.hc["pat"], name="c_pat").ap()
        d["c_ones16"] = nc.inline_tensor(np.ones((128, 128), BFNP), name="c_ones16").ap()
        d["c_ones32"] = nc.inline_tensor(np.ones((128, 128), np.float32), name="c_ones32").ap()
        d["c_id16"] = nc.inline_tensor(np.eye(128, dtype=BFNP), name="c_id16").ap()
        d["c_id32"] = nc.inline_tensor(np.eye(128, dtype=np.float32), name="c_id32").ap()
        d["c_zeros"] = nc.inline_tensor(np.zeros((128, 8), np.float32), name="c_zeros").ap()
        d["c_zeros16"] = nc.inline_tensor(np.zeros((128, 8), BFNP), name="c_zeros16").ap()
        sel = np.zeros((8, 1024), BFNP)
        for e in range(8):
            sel[e, e * 128:(e + 1) * 128] = 1
        d["c_sel"] = nc.inline_tensor(sel, name="c_sel").ap()
        d["c_n30"] = nc.inline_tensor(np.full((128, 1), -30.0, np.float32),
                                      name="c_n30").ap()
        self.d = d

    def _sbuf(self):
        nc = self.nc
        s = {}
        self._off = (nc.sbuf_base + 31) & ~31

        def at(name, shape, dtype, offset=None):
            per_part = int(np.prod(shape[1:])) * dtsize(dtype)
            per_part = (per_part + 31) & ~31
            if offset is None:
                offset = self._off
                self._off = offset + per_part
            s[name] = nc.alloc_sbuf_tensor_at(name, shape, dtype, offset=offset)
            return offset + per_part

        at("resid", [128, NCH, T], F32)
        xl0 = self._off
        at("x_ln", [128, NCH, T], BF16)
        gb0 = self._off
        at("gbuf", [128, NCH, T], BF16)
        wa0 = self._off
        self._off += 64 * 1024
        wa_end = self._off

        o = wa0
        o = at("w_q", [128, NCH, C], BF16, o)
        o = at("w_k", [128, NCH, C], BF16, o)
        o = at("w_v", [128, NCH, C], BF16, o)
        o = at("w_g", [128, NCH, C], BF16, o)
        assert o <= wa_end
        o = wa0
        o = at("w_ffn0", [128, 12288], BF16, o)
        o = at("w_ffn1", [128, 12288], BF16, o)
        o = at("w_gate", [128, NCH, E_FFN], BF16, o)
        o = at("f_sg", [128, 4, 512], BF16, o)
        o = at("sel16_sb", [8, 1024], BF16, o)
        o = at("cb_fm16", [8, 1024], BF16, o)
        o = at("cb_bc", [128, 2, 512], BF16, o)
        lg0 = o
        o = at("lg_sb", [8, 128], F32, o)
        at("gw_stg", [128, E_FFN], BF16, lg0)
        o = at("cb_tm", [128, NCHUNK, E_FFN], F32, o)
        o = at("lgT", [128, NCHUNK, E_FFN], F32, o)
        o = at("gw_sc", [128, NCHUNK, E_FFN], F32, o)
        o = at("gw_tmp", [128, NCHUNK, E_FFN], F32, o)
        o = at("gw_sm", [128, NCHUNK, E_FFN], F32, o)
        o = at("gw_msk", [128, NCHUNK, E_FFN], F32, o)
        o = at("gw_z", [128, NCHUNK, E_FFN], F32, o)
        o = at("gw_key", [128, NCHUNK, E_FFN], F32, o)
        o = at("gw_gs", [128, NCHUNK, 4], F32, o)
        o = at("gw_t1", [128, NCHUNK, 4], F32, o)
        o = at("gw_g2", [128, NCHUNK, 4], F32, o)
        o = at("gw_t2", [128, NCHUNK, 4], F32, o)
        o = at("gw_h2", [128, NCHUNK, 2], F32, o)
        o = at("gw_m1", [128, NCHUNK, 1], F32, o)
        o = at("gw_m2", [128, NCHUNK, 1], F32, o)
        o = at("gw_se", [128, NCHUNK, 1], F32, o)
        o = at("gw_rs", [128, NCHUNK, 1], F32, o)
        assert o <= wa_end, (o, wa_end)
        o = wa0
        o = at("wb_q", [128, NCH, 256], BF16, o)
        o = at("wb_k", [128, NCH, 256], BF16, o)
        o = at("wb_v", [128, NCH, 256], BF16, o)
        o = at("wb_o", [128, NCH, C], BF16, o)
        o = at("qf_b", [128, 2, T], BF16, o)
        o = at("kf_b", [128, 2, T], BF16, o)
        o = at("vt_b", [128, NCHUNK, 260], BF16, o)
        o = at("exps", [128, 8, 256], BF16, o)
        o = at("b_rz", [128, 512], F32, o)
        o = at("b_rzbc", [128, 512], F32, o)
        o = at("att_tmp", [128, 512], BF16, o)
        assert o <= wa_end, (o, wa_end)

        o = gb0
        o = at("w_o_a", [128, NCH, C], BF16, o)
        o = at("a_A", [128, NCH, 257], F32, o)
        o = at("a_g", [128, NCH, L], F32, o)
        o = at("mix2", [128, NCH, L], BF16, o)
        assert o <= wa0, (o, wa0)
        at("att_b", [128, NCH, T], BF16, gb0)
        at("f_y", [128, NCH, 1024], F32, gb0)

        o = xl0
        o = at("warm", [128, 512], F32, o)
        o = at("a_qexp", [128, NCH, L], BF16, xl0)
        o = at("a_kexp", [128, NCH, L], BF16, o)
        o = at("a_vaug", [128, HA, 257], BF16, o)
        o = at("a_var", [128, 3, NCH, L], BF16, o)
        o = at("a_Abf", [128, 2, 257], BF16, o)
        o = at("a_kutm", [128, 2, L], BF16, o)
        o = at("a_att", [128, NCH, L], BF16, o)
        o = at("a_ku", [128, NCH, L], BF16, o)
        o = at("t_pow", [128, NCH, 130], BF16, o)
        o = at("xw16", [128, 2, NCH, 132], BF16, o)
        o = at("xsq16", [128, NCH, 132], BF16, o)
        o = at("xlnw", [128, NCH, 132], BF16, o)
        o = at("mix_s", [128, 1, NCH, L], BF16, o)
        o = at("mixd", [128, NCH, L], BF16, o)
        assert o <= gb0, (o, gb0)

        at("c_ones16_sb", [128, 128], BF16)
        at("c_ones32_sb", [128, 128], F32)
        at("c_id16_sb", [128, 128], BF16)
        at("c_id32_sb", [128, 128], F32)
        at("c_zeros_sb", [128, 8], F32)
        at("c_masks_sb", [128, 10, 128], U8)
        at("c_pat_sb", [128, 132], F32)
        at("a_small_sb", [128, NCH, 6], F32)
        at("bqk_sb", [128, NCH, 2], F32)
        at("bo_sb", [128, NCH, 1], F32)
        at("bv_bc_sb", [128, C], F32)
        at("e_bias_sb", [128, E_FFN], F32)
        at("eidx_sb", [128, E_FFN], F32)
        at("ln_m", [128, 132], F32)
        at("ln_r", [128, 132], F32)
        at("scr8", [128, NCH, 132], F32)
        at("a_pt", [128, 4, 128], BF16)
        at("rz", [1, 128], F32)
        at("rzbc", [128, 128], F32)
        at("f_cbT", [128, E_FFN], F32)
        at("f_sc", [128, E_FFN], F32)
        at("f_tmp", [128, E_FFN], F32)
        at("f_tmp2", [128, E_FFN], F32)
        at("f_gs", [128, 4], F32)
        at("f_key", [128, E_FFN], F32)
        at("rmsq", [128, 128], F32)
        at("c_n30_sb", [128, 1], F32)
        at("t_dk", [128, NCH, 2], BF16)
        at("t_logw", [128, NCH], F32)
        at("t_wL", [128, NCH], F32)
        assert self._off <= 229376, self._off
        self.s = s

    def _psum(self):
        nc = self.nc
        self.ps = [nc.alloc_psum_tensor(f"ps{i}", [128, 512], F32) for i in range(8)]
        self.ps16 = self.ps[7].ap().bitcast(BF16) if False else None

    def V(self, method, *a, waits=(), **k):
        self.p.rec("vector", method, *a, waits=waits, inc="v", **k)
        return ("v", self.p.now("v"))

    def S(self, method, *a, waits=(), **k):
        self.p.rec("scalar", method, *a, waits=waits, inc="s", **k)
        return ("s", self.p.now("s"))

    def MM(self, *a, waits=(), inc_group=False, **k):
        self.p.rec("tensor", "matmul", *a, waits=waits,
                   inc="t" if inc_group else None, **k)
        return ("t", self.p.now("t"))

    def TR(self, *a, waits=(), inc_group=False, **k):
        self.p.rec("tensor", "transpose", *a, waits=waits,
                   inc="t" if inc_group else None, **k)
        return ("t", self.p.now("t"))

    def bank_w(self, i):
        return [self.bankmark[i]] if i in self.bankmark else []

    def bank_done(self, i, mark):
        self.bankmark[i] = mark

    def gen_consts(self):
        p, d, s = self.p, self.d, self.s
        p.rec("sync", "dma_start", out=s["c_ones16_sb"][:], in_=d["c_ones16"][:, :], inc="y", incn=16)
        p.rec("sync", "dma_start", out=s["c_ones32_sb"][:], in_=d["c_ones32"][:, :], inc="y", incn=16)
        p.rec("sync", "dma_start", out=s["c_id16_sb"][:], in_=d["c_id16"][:, :], inc="y", incn=16)
        p.rec("sync", "dma_start", out=s["c_id32_sb"][:], in_=d["c_id32"][:, :], inc="y", incn=16)
        p.rec("sync", "dma_start", out=s["c_zeros_sb"][:], in_=d["c_zeros"][:, :], inc="y", incn=16)
        p.rec("sync", "dma_start", out=s["c_masks_sb"][:, :, :],
              in_=d["c_masks"][:, :].rearrange("a (p b) -> p a b", p=128), inc="y", incn=16)
        p.rec("sync", "dma_start", out=s["c_pat_sb"][:], in_=d["c_pat"][:, :], inc="y", incn=16)
        p.rec("sync", "dma_start", out=s["a_small_sb"][:, :, :],
              in_=d["a_small"][:, :].rearrange("(j p) k -> p j k", p=128), inc="y", incn=16,
              allow_slow_non_contiguous=True)
        p.rec("sync", "dma_start", out=s["bqk_sb"][:, :, :],
              in_=d["bqkT"][:, :].rearrange("(j p) k -> p j k", p=128), inc="y", incn=16,
              allow_slow_non_contiguous=True)
        p.rec("sync", "dma_start", out=s["bo_sb"][:, :, :],
              in_=d["bo_b"][:, :].rearrange("(j p) k -> p j k", p=128), inc="y", incn=16,
              allow_slow_non_contiguous=True)
        p.rec("sync", "dma_start", out=s["bv_bc_sb"][:], in_=d["bv_bc"][:, :], inc="y", incn=16)
        p.rec("sync", "dma_start", out=s["c_n30_sb"][:], in_=d["c_n30"][:, :], inc="y", incn=16)
        p.rec("sync", "dma_start", out=s["e_bias_sb"][:], in_=d["e_bias_bc"][:, :], inc="y", incn=16)
        p.rec("sync", "dma_start", out=s["eidx_sb"][:], in_=d["eidx_bc"][:, :], inc="y", incn=16)
        self.cld = ("y", p.now("y"))

    def gen_warmup(self):
        s = self.s
        w = [self.cld]
        wsrc = s["c_ones32_sb"][:, 0:128].unsqueeze(1).broadcast_to([128, 4, 128])
        for i in range(16):
            self.V("tensor_copy", s["warm"][:, 0:512].rearrange("p (a b) -> p a b", b=128),
                   wsrc, waits=w if i == 0 else ())
        mv = self.V("tensor_copy", s["warm"][:, 0:128], s["c_ones32_sb"][:])
        for i in range(4):
            ms = self.S("activation", s["warm"][:, 128:256], s["c_ones32_sb"][:], AF.Exp,
                        waits=w if i == 0 else ())
        for i in range(6):
            mt = self.MM(self.ps[0][:, 0:128], s["c_ones16_sb"][:], s["c_ones16_sb"][:],
                         start=True, stop=True, waits=w if i == 0 else (),
                         inc_group=(i == 5))
        mg = ("g", self.p.now("g"))
        bar = [mv, ms, mt, self.cld]
        for e in ("vector", "scalar", "tensor", "sync", "gpsimd"):
            self.p.rec(e, "_wait_only", waits=bar)
        self.bank_done(0, mv)

    def gen_tables(self):
        p, s = self.p, self.s
        w = [self.cld, ("v", p.now("v")), ("t", p.now("t"))]
        self.S("activation", s["t_logw"][:, :].unsqueeze(2), s["a_small_sb"][:, :, 0:1],
               AF.Exp, waits=w)
        mk = self.V("tensor_scalar_mul", s["t_logw"][:, :], s["t_logw"][:, :], -1.0,
                    waits=[("s", p.now("s"))])
        self.S("activation", s["t_wL"][:, :].unsqueeze(2), s["t_logw"][:, :].unsqueeze(2),
               AF.Exp, scale=float(L), waits=[mk])
        offs = self.hc["pat_offs"]
        logw_b = s["t_logw"][:, :].unsqueeze(2)

        def build(dst_base, pat_off, plen):
            done = 0
            while done < plen:
                n = min(128, plen - done)
                pat = s["c_pat_sb"][:, pat_off + done:pat_off + done + n] \
                    .unsqueeze(1).broadcast_to([128, NCH, n])
                lw = logw_b.broadcast_to([128, NCH, n])
                mv = self.V("tensor_tensor", s["scr8"][:, :, 0:n], pat, lw, ALU.mult,
                            waits=[("s", p.now("s"))])
                self.S("activation", dst_base[:, :, done:done + n], s["scr8"][:, :, 0:n],
                       AF.Exp, waits=[mv])
                done += n

        build(s["t_pow"], offs[0], 130)
        build(s["t_dk"], offs[1], 2)
        self.tables_done = ("s", p.now("s"))

    def tpow(self, lo, hi, rev=False):
        if not rev:
            return self.s["t_pow"][:, :, lo:hi]
        return self.s["t_pow"][:, :, hi - 1:(None if lo == 0 else lo - 1):-1]

    def gen_load_x(self, batch, waits=()):
        p, d, s = self.p, self.d, self.s
        p.rec("sync", "dma_start", out=s["resid"][:, :, :],
              in_=d["x"][batch].rearrange("(j p) t -> p j t", p=128),
              waits=waits, inc="y", incn=16)
        self.xld = ("y", p.now("y"))

    def gen_attn_a_weights(self, waits=()):
        p, d, s = self.p, self.d, self.s
        for nm, dst in (("wqT", "w_q"), ("wkT", "w_k"), ("wvT", "w_v"), ("wgT", "w_g")):
            p.rec("gpsimd", "dma_start", out=s[dst][:, :, :],
                  in_=d[nm][:, :].rearrange("(j p) o -> p j o", p=128),
                  waits=waits, inc="g", incn=16)
        p.rec("gpsimd", "dma_start", out=s["w_o_a"][:, :, :],
              in_=d["woT"][:, :].rearrange("(j p) o -> p j o", p=128),
              waits=waits, inc="g", incn=16)
        self.aw = ("g", p.now("g"))

    def gen_attn_a(self, batch):
        p, s, d = self.p, self.s, self.d
        wts = [self.xld, self.aw, self.tables_done,
               getattr(self, "vaug_cons", ("t", 0)), ("v", p.now("v"))]
        for h in range(HA):
            p.rec("sync", "dma_start", out=s["a_vaug"][:, h, 256:257],
                  in_=d["c_ones16"][:, 0:1], waits=wts if h == 0 else (),
                  inc="y", incn=16, allow_slow_non_contiguous=True)
        p.rec("sync", "dma_start", out=s["xw16"][:, 0, :, 0:1],
              in_=d["c_zeros16"][:, 0:NCH].unsqueeze(2), inc="y", incn=16,
              allow_slow_non_contiguous=True)
        self.vaug_ones = ("y", p.now("y"))
        self.prev0 = ("y", p.now("y"))
        self.win_mark = self.V("tensor_copy", s["xw16"][:, 0, :, 1:129],
                               s["resid"][:, :, 0:L],
                               waits=[self.xld, self.prev0, self.tables_done, self.aw,
                                      getattr(self, "attA", ("v", 0))])
        self.Aup = ("t", 0)
        self.attA = ("v", 0) if batch == 0 else self.attA
        self.var_marks = [None] * 3
        self.var_cons = [("t", 0)] * 3
        self.var_i = 0
        self.mix_cons = [("t", 0), ("t", 0)]
        for c in range(NCHUNK):
            self._attn_a_chunk(batch, c)

    def _emit_var(self, kind, idx, trim, src_mark):
        p, s = self.p, self.s
        if kind in ("lk", "dk"):
            slot = 2
        else:
            slot = self.var_i % 2
            self.var_i += 1
        var = s["a_var"]
        waits = [src_mark, self.var_cons[slot]]
        if kind == "q1":
            tab = self.tpow(1, 129)
            mk = self.V("tensor_mul", var[:, slot, :, :], s["a_qexp"][:, :, :], tab,
                        waits=waits)
        elif kind == "lq":
            lsz = (32, 32, 32, 8, 8, 8, 2, 2, 2)[idx]
            delta = (1, 2, 3)[idx % 3]
            nblk = (L - trim) // lsz
            lo = (delta - 1) * lsz
            tab = self.tpow(lo, lo + lsz).unsqueeze(2).broadcast_to([128, NCH, nblk, lsz])
            srcv = s["a_qexp"][:, :, trim:].rearrange("p j (a b) -> p j a b", b=lsz)
            dst = var[:, slot, :, 0:L - trim].rearrange("p j (a b) -> p j a b", b=lsz)
            mk = self.V("tensor_mul", dst, srcv, tab, waits=waits)
        elif kind == "dq":
            tab = self.tpow(0, 2).unsqueeze(2).broadcast_to([128, NCH, 64, 2])
            srcv = s["a_qexp"][:, :, :].rearrange("p j (a b) -> p j a b", b=2)
            dst = var[:, slot, :, :].rearrange("p j (a b) -> p j a b", b=2)
            mk = self.V("tensor_mul", dst, srcv, tab, waits=waits)
        elif kind == "lk":
            lsz = (32, 8, 2)[idx]
            tab = self.tpow(1, lsz + 1, rev=True).unsqueeze(2).broadcast_to([128, NCH, L // lsz, lsz])
            srcv = s["a_kexp"][:, :, :].rearrange("p j (a b) -> p j a b", b=lsz)
            dst = var[:, slot, :, :].rearrange("p j (a b) -> p j a b", b=lsz)
            mk = self.V("tensor_mul", dst, srcv, tab, waits=waits)
        elif kind == "dk":
            tab = s["t_dk"][:, :, :].unsqueeze(2).broadcast_to([128, NCH, 64, 2])
            srcv = s["a_kexp"][:, :, :].rearrange("p j (a b) -> p j a b", b=2)
            dst = var[:, slot, :, :].rearrange("p j (a b) -> p j a b", b=2)
            mk = self.V("tensor_mul", dst, srcv, tab, waits=waits)
        self.var_marks[slot] = mk
        return slot

    def _attn_a_chunk(self, batch, c):
        p, s, ps = self.p, self.s, self.ps
        self.tap(f"wg_at_c{c}", s["w_g"][:, :, :], [self.aw, ("v", p.now("v")), ("t", p.now("t")), ("s", p.now("s"))])
        ones16 = s["c_ones16_sb"]
        t0 = c * L
        wlen = 129
        slot_w = c % 2
        xw = s["xw16"][:, slot_w, :, :]

        if c + 1 < NCHUNK:
            nxt = s["xw16"][:, (c + 1) % 2, :, :]
            self.win_mark_next = self.V(
                "tensor_copy", nxt[:, :, 0:129],
                s["resid"][:, :, (c + 1) * L - 1:(c + 1) * L + L],
                waits=[self.win_mark, self.attA])

        m_xsq = self.V("tensor_mul", s["xsq16"][:, :, 0:wlen], xw[:, :, 0:wlen],
                       xw[:, :, 0:wlen], waits=[self.win_mark])
        for j in range(NCH):
            self.MM(ps[2][:, 0:wlen], ones16[:], xw[:, j, 0:wlen],
                    start=(j == 0), stop=(j == NCH - 1),
                    waits=([self.win_mark] + self.bank_w(2)) if j == 0 else ())
        for j in range(NCH):
            t_ln = self.MM(ps[2][:, 136:136 + wlen], ones16[:], s["xsq16"][:, j, 0:wlen],
                           start=(j == 0), stop=(j == NCH - 1),
                           waits=[m_xsq] if j == 0 else (), inc_group=(j == NCH - 1))
        self.V("tensor_scalar_mul", s["ln_m"][:, 0:wlen], ps[2][:, 0:wlen], 1.0 / C,
               waits=[t_ln])
        self.V("tensor_mul", s["ln_r"][:, 0:wlen], s["ln_m"][:, 0:wlen], s["ln_m"][:, 0:wlen])
        m_var = self.V("scalar_tensor_tensor", s["ln_r"][:, 0:wlen], ps[2][:, 136:136 + wlen],
                       1.0 / C, s["ln_r"][:, 0:wlen], ALU.mult, ALU.subtract)
        m_var = self.V("tensor_scalar_add", s["ln_r"][:, 0:wlen], s["ln_r"][:, 0:wlen], 1e-5)
        self.S("activation", s["ln_r"][:, 0:wlen], s["ln_r"][:, 0:wlen], AF.Ln,
               waits=[m_var])
        m_rstd = self.S("activation", s["ln_r"][:, 0:wlen], s["ln_r"][:, 0:wlen], AF.Exp,
                        scale=-0.5)
        mb = s["ln_m"][:, 0:wlen].unsqueeze(1).broadcast_to([128, NCH, wlen])
        rb = s["ln_r"][:, 0:wlen].unsqueeze(1).broadcast_to([128, NCH, wlen])
        self.V("tensor_sub", s["scr8"][:, :, 0:wlen], xw[:, :, 0:wlen], mb,
               waits=[m_rstd])
        m_xln = self.V("tensor_mul", s["xlnw"][:, :, 0:wlen], s["scr8"][:, :, 0:wlen], rb)
        self.bank_done(2, m_var)
        self.tap(f"xlnw_c{c}", s["xlnw"][:, :, 1:129], [m_xln])
        self.tap(f"xlnw0_c{c}", s["xlnw"][:, :, 0:1], [m_xln])
        self.tap(f"win_c{c}", xw[:, :, 0:129], [self.win_mark if c + 1 < NCHUNK else m_xln])
        self.win_mark = getattr(self, "win_mark_next", None)

        xln1 = s["xlnw"][:, :, 1:129]
        xln0 = s["xlnw"][:, :, 0:128]
        m_d = self.V("tensor_sub", s["mixd"][:, :, :], xln1, xln0, waits=[m_xln])

        marks = {}
        projs = [("q", "w_q", 1, (0, 1)), ("k", "w_k", 2, (3, 4)),
                 ("v", "w_v", 3, (0, 1)), ("g", "w_g", 4, (3, 4))]

        def emit_mix(pi):
            nm, _, mix_i, _ = projs[pi]
            slot_i = pi % 2
            slot = s["mix_s"][:, 0, :, :] if slot_i == 0 else s["mix2"][:, :, :]
            mcol = s["a_small_sb"][:, :, mix_i:mix_i + 1].broadcast_to([128, NCH, L])
            self.V("tensor_mul", slot, s["mixd"][:, :, :], mcol,
                   waits=[m_d, self.mix_cons[slot_i]])
            m_mix = self.V("tensor_add", slot, slot, xln0)
            self.tap(f"mix{nm}_c{c}", slot, [m_mix])
            return slot, m_mix

        nxt_mix = emit_mix(0)
        for pi, (nm, wname, mix_i, (bA, bB)) in enumerate(projs):
            slot_i = pi % 2
            slot, m_mix = nxt_mix
            wgt = s[wname]
            if nm != "v":
                for m in range(8):
                    bank = ps[bA] if m < 4 else ps[bB]
                    cols = (m % 4) * 128
                    for kk in range(NCH):
                        t_mk = self.MM(bank[:, cols:cols + 128],
                                       wgt[:, kk, m * 128:(m + 1) * 128], slot[:, kk, :],
                                       start=(kk == 0), stop=(kk == NCH - 1),
                                       waits=([m_mix] + self.bank_w(bA) + self.bank_w(bB))
                                       if (m == 0 and kk == 0) else (),
                                       inc_group=(m == 7 and kk == NCH - 1))
            else:
                for half in range(2):
                    bank = ps[bA] if half == 0 else ps[bB]
                    for kk in range(NCH):
                        t_mk = self.MM(bank[:, 0:512],
                                       slot[:, kk, :], wgt[:, kk, half * 512:(half + 1) * 512],
                                       start=(kk == 0), stop=(kk == NCH - 1),
                                       waits=([m_mix] + self.bank_w(bA) + self.bank_w(bB))
                                       if (half == 0 and kk == 0) else (),
                                       inc_group=(half == 1 and kk == NCH - 1))
            self.mix_cons[slot_i] = t_mk
            marks[nm] = (t_mk, bA, bB)
            if pi + 1 < len(projs):
                nxt_mix = emit_mix(pi + 1)

            if nm in ("q", "k"):
                dst = s["a_qexp"] if nm == "q" else s["a_kexp"]
                self.V("tensor_copy", dst[:, 0:4, :].rearrange("p a b -> p (a b)"),
                       ps[bA][:, 0:512], waits=[t_mk])
                m_ev = self.V("tensor_copy", dst[:, 4:8, :].rearrange("p a b -> p (a b)"),
                              ps[bB][:, 0:512])
                self.bank_done(bA, m_ev)
                self.bank_done(bB, m_ev)
                m_sq = self.V("tensor_mul", s["xsq16"][:, :, 0:L], dst[:, :, :], dst[:, :, :])
                for kk in range(NCH):
                    t_rms = self.MM(ps[2][:, 272:400], ones16[:], s["xsq16"][:, kk, 0:L],
                                    start=(kk == 0), stop=(kk == NCH - 1),
                                    waits=([m_sq] + self.bank_w(2)) if kk == 0 else (),
                                    inc_group=(kk == NCH - 1))
                self.S("activation", s["rmsq"][:], ps[2][:, 272:400], AF.Ln, scale=1.0 / C,
                       waits=[t_rms])
                m_rms = self.S("activation", s["rmsq"][:], s["rmsq"][:], AF.Exp, scale=0.5)
                self.V("tensor_scalar_add", s["rmsq"][:], s["rmsq"][:], 1e-8, waits=[m_rms])
                m_rec = self.V("reciprocal", s["rmsq"][:], s["rmsq"][:])
                self.bank_done(2, m_rec)
                rbc = s["rmsq"][:].unsqueeze(1).broadcast_to([128, NCH, L])
                self.V("tensor_mul", s["scr8"][:, :, 0:L], dst[:, :, :], rbc,
                       waits=[marks["q_exp"]] if nm == "k" else ())
                nscb = s["a_small_sb"][:, :, 5:6].broadcast_to([128, NCH, L])
                m_qr = self.V("tensor_mul", s["scr8"][:, :, 0:L], s["scr8"][:, :, 0:L], nscb)
                m_exp = self.S("activation", dst[:, :, :], s["scr8"][:, :, 0:L], AF.Exp,
                               waits=[m_qr])
                marks[nm + "_exp"] = m_exp
                self.tap(f"{nm}exp_c{c}", dst[:, :, :], [m_exp])
            elif nm == "v":
                self.V("tensor_copy",
                       s["a_vaug"][:, 0:2, 0:256],
                       ps[bA][:, 0:512].rearrange("p (a b) -> p a b", b=256),
                       waits=[t_mk, getattr(self, "vaug_cons", ("t", 0))])
                m_ev = self.V("tensor_copy",
                              s["a_vaug"][:, 2:4, 0:256],
                              ps[bB][:, 0:512].rearrange("p (a b) -> p a b", b=256))
                self.bank_done(bA, m_ev)
                self.bank_done(bB, m_ev)
                marks["vaug"] = m_ev
            else:
                if f"gps_c{c}" in self.taps:
                    mdbg = self.V("tensor_copy", s["warm"][:, 0:512], ps[bA][:, 0:512],
                                  waits=[t_mk])
                    self.tap(f"gps_c{c}", s["warm"][:, 0:512], [mdbg])
                self.S("activation", s["a_g"][:, 0:4, :],
                       ps[bA][:, 0:512].rearrange("p (a b) -> p a b", b=L), AF.Sigmoid,
                       waits=[t_mk, ("v", p.now("v"))])
                m_g = self.S("activation", s["a_g"][:, 4:8, :],
                             ps[bB][:, 0:512].rearrange("p (a b) -> p a b", b=L),
                             AF.Sigmoid)
                self.bank_done(bA, m_g)
                self.bank_done(bB, m_g)
                marks["g"] = m_g
                self.tap(f"g_c{c}", s["a_g"][:, :, :], [m_g])
                self.tap(f"vaug_c{c}", s["a_vaug"][:, :, :], [marks["vaug"]])
                self.tap(f"xlnw_c{c}", s["xlnw"][:, :, :], [m_xln])

        pt_cons = getattr(self, "pt_cons", [("t", 0)] * 4)
        zb = s["c_zeros_sb"][:, 0:1].unsqueeze(1).broadcast_to([128, 4, 128])
        m_ptz = self.V("tensor_copy", s["a_pt"][:, :, :], zb, waits=pt_cons)

        ev_marks = {h: [] for h in range(HA)}
        piece_i = 0
        piece_plan = []
        for level_i, lsz in enumerate((32, 8, 2)):
            piece_plan.append(("lk", level_i, 0, None))
            for di, delta in enumerate((1, 2, 3)):
                piece_plan.append(("lq", level_i * 3 + di, delta * lsz, piece_i))
                piece_i += 1
        piece_plan.append(("dk", 0, 0, None))
        piece_plan.append(("dq", 0, 0, 9))

        kslot = None
        for (kind, idx, trim, pidx) in piece_plan:
            if kind in ("lk", "dk"):
                kslot = self._emit_var(kind, idx, 0, marks["k_exp"])
                self.tap(f"var_{kind}{idx}_c{c}", s["a_var"][:, kslot, :, :],
                         [self.var_marks[kslot]])
                continue
            qslot = self._emit_var(kind, idx, trim, marks["q_exp"])
            self.tap(f"var_{kind}{idx}_c{c}", s["a_var"][:, qslot, :, :],
                     [self.var_marks[qslot]])
            N = L - trim
            for h in range(HA):
                bank = 5 if (pidx * HA + h) % 2 == 0 else 6
                for kk in range(2):
                    t_p = self.MM(ps[bank][:, 0:N],
                                  s["a_var"][:, kslot, 2 * h + kk, :],
                                  s["a_var"][:, qslot, 2 * h + kk, 0:N],
                                  start=(kk == 0), stop=(kk == 1),
                                  waits=([self.var_marks[kslot], self.var_marks[qslot]]
                                         + self.bank_w(bank)) if kk == 0 else (),
                                  inc_group=(kk == 1))
                self.var_cons[kslot] = t_p
                self.var_cons[qslot] = t_p
                m_cp = self.V("copy_predicated", s["a_pt"][:, h, trim:trim + N],
                              s["c_masks_sb"][:, piece_def_idx(pidx), trim:trim + N]
                              if False else s["c_masks_sb"][:, pidx, trim:trim + N],
                              ps[bank][:, 0:N],
                              waits=[t_p, m_ptz])
                self.bank_done(bank, m_cp)
                ev_marks[h].append(m_cp)

        q1slot = self._emit_var("q1", 0, 0, marks["q_exp"])
        m_ku = self.V("tensor_mul", s["a_ku"][:, :, :], s["a_kexp"][:, :, :],
                      self.tpow(0, L, rev=True),
                      waits=[marks["k_exp"], getattr(self, "ku_cons", ("t", 0))])

        att_marks = []
        pt_next = []
        if not hasattr(self, "_rz_cons"):
            self._rz_cons = {}
            self._head_ts = ("t", 0)
        hstate = {}
        SKEW = True
        if SKEW:
            for step in range(HA + 1):
                if step < HA:
                    hstate[step] = self._head_p1(c, step, q1slot, ev_marks[step],
                                                 marks)
                if step >= 1:
                    m_att, m_ptfree = self._head_p2(c, step - 1, hstate[step - 1],
                                                    marks, m_ku)
                    att_marks.append(m_att)
                    pt_next.append(m_ptfree)
        else:
            for h in range(HA):
                st = self._head_p1(c, h, q1slot, ev_marks[h], marks)
                m_att, m_ptfree = self._head_p2(c, h, st, marks, m_ku)
                att_marks.append(m_att)
                pt_next.append(m_ptfree)
        self.pt_cons = pt_next
        self.tap(f"A_c{c}", s["a_A"][:, :, :], [self.Aup])

        for m in range(8):
            bank = 5 if m < 4 else 6
            cols = (m % 4) * 128
            for kk in range(NCH):
                t_wo = self.MM(ps[bank][:, cols:cols + 128],
                               s["w_o_a"][:, kk, m * 128:(m + 1) * 128], s["a_att"][:, kk, :],
                               start=(kk == 0), stop=(kk == NCH - 1),
                               waits=(att_marks + self.bank_w(5) + self.bank_w(6))
                               if (m == 0 and kk == 0) else (),
                               inc_group=(m == 7 and kk == NCH - 1))
        self.V("tensor_add", s["resid"][:, 0:4, t0:t0 + L],
               s["resid"][:, 0:4, t0:t0 + L],
               ps[5][:, 0:512].rearrange("p (a b) -> p a b", b=L), waits=[t_wo])
        m_res = self.V("tensor_add", s["resid"][:, 4:8, t0:t0 + L],
                       s["resid"][:, 4:8, t0:t0 + L],
                       ps[6][:, 0:512].rearrange("p (a b) -> p a b", b=L))
        self.bank_done(5, m_res)
        self.bank_done(6, m_res)
        self.attA = m_res

    def _head_p1(self, c, h, q1slot, ev_marks_h, marks):
        p, s, ps = self.p, self.s, self.ps
        first = (c == 0)
        b0 = 0 if h % 2 == 0 else 2
        bbc = 1 if h % 2 == 0 else 5
        SA = ps[b0][:, 0:128]
        SB = ps[b0][:, 128:256]
        SZ = ps[b0][0:1, 256:384]
        rz_ap = s["rz"][0:1, 0:128] if h % 2 == 0 else s["scr8"][0:1, 2, 0:128]
        rzbc_ap = s["rzbc"][:] if h % 2 == 0 else s["scr8"][:, 1, 0:128]
        var = s["a_var"]
        w0 = [self.var_marks[q1slot]] + self.bank_w(b0)
        if not first:
            m_Abf = self.V("tensor_copy", s["a_Abf"][:, :, :],
                           s["a_A"][:, 2 * h:2 * h + 2, :],
                           waits=[self.Aup, self._head_ts])
            w0 = w0 + [m_Abf]
            for m in range(2):
                dst = SA if m == 0 else SB
                for kk in range(2):
                    self.MM(dst, s["a_Abf"][:, kk, m * 128:(m + 1) * 128],
                            var[:, q1slot, 2 * h + kk, :],
                            start=(kk == 0), stop=False,
                            waits=w0 if (m == 0 and kk == 0) else ())
            for kk in range(2):
                self.MM(SZ, s["a_Abf"][:, kk, 256:257], var[:, q1slot, 2 * h + kk, :],
                        start=(kk == 0), stop=False)
        wpt = ev_marks_h + [self.vaug_ones, marks["vaug"]] + (w0 if first else [])
        self.MM(SA, s["a_vaug"][:, h, 0:128], s["a_pt"][:, h, :],
                start=first, stop=True, waits=wpt)
        self.MM(SB, s["a_vaug"][:, h, 128:256], s["a_pt"][:, h, :],
                start=first, stop=True)
        t_S = self.MM(SZ, s["a_vaug"][:, h, 256:257], s["a_pt"][:, h, :],
                      start=first, stop=True, inc_group=True)
        self.var_cons[q1slot] = t_S
        self._head_ts = t_S
        m_z = self.V("tensor_copy", rz_ap, SZ,
                     waits=[t_S, self._rz_cons.get(h % 2, ("t", 0))])
        t_bc = self.MM(ps[bbc][:, 0:128], s["c_ones32_sb"][0:1, :], rz_ap,
                       start=True, stop=True, waits=[m_z] + self.bank_w(bbc),
                       inc_group=True)
        self._rz_cons[h % 2] = t_bc
        m_rzbc = self.V("reciprocal", rzbc_ap, ps[bbc][:, 0:128], waits=[t_bc])
        self.bank_done(bbc, m_rzbc)
        return (b0, rzbc_ap, t_S, m_rzbc)

    def _head_p2(self, c, h, state, marks, m_ku):
        p, s, ps = self.p, self.s, self.ps
        first = (c == 0)
        b0, rzbc_ap, t_S, m_rzbc = state
        SA = ps[b0][:, 0:128]
        SB = ps[b0][:, 128:256]
        m_att = None
        for m in range(2):
            src = SA if m == 0 else SB
            jj = 2 * h + m
            self.V("tensor_mul", s["scr8"][:, 0, 0:L], src, s["a_g"][:, jj, :],
                   waits=[m_rzbc, marks["g"]])
            m_att = self.V("tensor_mul", s["a_att"][:, jj, :], s["scr8"][:, 0, 0:L],
                           rzbc_ap)
        self.bank_done(b0, m_att)

        for kk in range(2):
            t_tr = self.TR(self.ps[7].bitcast(BF16)[:, kk * 128:(kk + 1) * 128],
                           s["a_ku"][:, 2 * h + kk, :], s["c_id16_sb"][:],
                           waits=([m_ku] + self.bank_w(7)) if kk == 0 else (),
                           inc_group=(kk == 1))
        self.V("tensor_copy", s["a_kutm"][:, 0, :], self.ps[7].bitcast(BF16)[:, 0:128],
               waits=[t_tr])
        m_kut = self.V("tensor_copy", s["a_kutm"][:, 1, :],
                       self.ps[7].bitcast(BF16)[:, 128:256])
        self.bank_done(7, m_kut)
        for m in range(2):
            bank = 3 if m == 0 else 4
            t_A = self.MM(ps[bank][:, 0:257], s["a_kutm"][:, m, :], s["a_vaug"][:, h, :],
                          start=True, stop=True,
                          waits=([m_kut, marks["vaug"]] + self.bank_w(3) + self.bank_w(4))
                          if m == 0 else (),
                          inc_group=(m == 1))
        m_A = None
        for m in range(2):
            bank = 3 if m == 0 else 4
            wL = s["t_wL"][:, 2 * h + m:2 * h + m + 1]
            if first:
                m_A = self.V("tensor_copy", s["a_A"][:, 2 * h + m, :],
                             ps[bank][:, 0:257], waits=[t_A] if m == 0 else ())
            else:
                m_A = self.V("scalar_tensor_tensor", s["a_A"][:, 2 * h + m, :],
                             s["a_A"][:, 2 * h + m, :], wL, ps[bank][:, 0:257],
                             ALU.mult, ALU.add, waits=[t_A] if m == 0 else ())
        self.bank_done(3, m_A)
        self.bank_done(4, m_A)
        self.Aup = m_A
        self.vaug_cons = t_A
        self.ku_cons = t_tr
        return m_att, t_S

    def phase_mark(self):
        p = self.p
        return [("t", p.now("t")), ("v", p.now("v")), ("s", p.now("s")),
                ("y", p.now("y")), ("g", p.now("g"))]

    def load_expert(self, e, slot_cons):
        p, d, s = self.p, self.d, self.s
        dst = s["w_ffn0"] if e % 2 == 0 else s["w_ffn1"]
        p.rec("gpsimd", "dma_start",
              out=dst[:, 0:8192].rearrange("p (two j o) -> p two j o", two=2, j=NCH),
              in_=d["ffn_w"][e, 0:2 * C * I_FFN].rearrange(
                  "(two j p o) -> p two j o", two=2, j=NCH, p=128),
              waits=slot_cons, inc="g", incn=16)
        p.rec("gpsimd", "dma_start",
              out=dst[:, 8192:12288].rearrange("p (j o) -> p j o", j=4),
              in_=d["ffn_w"][e, 2 * C * I_FFN:].rearrange("(j p o) -> p j o", j=4, p=128),
              inc="g", incn=16)
        return ("g", p.now("g"))

    def _ln_chunk_to(self, dst_ap, src_ap, waits, t0):
        p, s, ps = self.p, self.s, self.ps
        ones32 = s["c_ones32_sb"]
        m_xsq = self.V("tensor_mul", s["scr8"][:, :, 0:L], src_ap, src_ap,
                       waits=waits)
        for j in range(NCH):
            self.MM(ps[2][:, 0:L], ones32[:], s["resid"][:, j, t0:t0 + L],
                    start=(j == 0), stop=(j == NCH - 1),
                    waits=([m_xsq] + self.bank_w(2)) if j == 0 else ())
        for j in range(NCH):
            t_ln = self.MM(ps[2][:, 136:136 + L], ones32[:], s["scr8"][:, j, 0:L],
                           start=(j == 0), stop=(j == NCH - 1),
                           waits=[m_xsq] if j == 0 else (), inc_group=(j == NCH - 1))
        self.V("tensor_scalar_mul", s["ln_m"][:, 0:L], ps[2][:, 0:L], 1.0 / C,
               waits=[t_ln])
        self.V("tensor_mul", s["ln_r"][:, 0:L], s["ln_m"][:, 0:L], s["ln_m"][:, 0:L])
        self.V("scalar_tensor_tensor", s["ln_r"][:, 0:L], ps[2][:, 136:136 + L],
               1.0 / C, s["ln_r"][:, 0:L], ALU.mult, ALU.subtract)
        m_var = self.V("tensor_scalar_add", s["ln_r"][:, 0:L], s["ln_r"][:, 0:L], 1e-5)
        self.bank_done(2, m_var)
        self.S("activation", s["ln_r"][:, 0:L], s["ln_r"][:, 0:L], AF.Ln, waits=[m_var])
        m_rstd = self.S("activation", s["ln_r"][:, 0:L], s["ln_r"][:, 0:L], AF.Exp,
                        scale=-0.5)
        mb = s["ln_m"][:, 0:L].unsqueeze(1).broadcast_to([128, NCH, L])
        rb = s["ln_r"][:, 0:L].unsqueeze(1).broadcast_to([128, NCH, L])
        self.V("tensor_sub", s["scr8"][:, :, 0:L], src_ap, mb, waits=[m_rstd])
        return self.V("tensor_mul", dst_ap, s["scr8"][:, :, 0:L], rb)

    def gen_ffn(self, batch, call):
        p, s, ps, d = self.p, self.s, self.ps, self.d
        pm = self.phase_mark()
        p.rec("gpsimd", "dma_start", out=s["w_gate"][:, :, :],
              in_=d["gate_wT"][:, :].rearrange("(j p) o -> p j o", p=128),
              waits=pm, inc="g", incn=16)
        p.rec("sync", "dma_start", out=s["sel16_sb"][:, :], in_=d["c_sel"][:, :],
              waits=pm, inc="y", incn=16)
        gatew = [("g", p.now("g")), ("y", p.now("y"))]
        slot_use = [pm, pm]
        eld = {}
        eld[0] = self.load_expert(0, pm)
        eld[1] = self.load_expert(1, pm)

        m_lgtr = None
        for c in range(NCHUNK):
            t0 = c * L
            m_xln = self._ln_chunk_to(s["x_ln"][:, :, t0:t0 + L],
                                      s["resid"][:, :, t0:t0 + L],
                                      pm if c == 0 else (), t0)
            for j in range(NCH):
                t_lg = self.MM(ps[0][0:8, 0:L], s["w_gate"][:, j, :],
                               s["x_ln"][:, j, t0:t0 + L],
                               start=(j == 0), stop=(j == NCH - 1),
                               waits=([m_xln] + gatew + self.bank_w(0)) if j == 0 else (),
                               inc_group=(j == NCH - 1))
            m_lg = self.V("tensor_copy", s["lg_sb"][:, :], ps[0][0:8, 0:L], waits=[t_lg])
            self.bank_done(0, m_lg)
            t_tr = self.TR(ps[1][:, 0:8], s["lg_sb"][:, :], s["c_id32_sb"][0:8, 0:8],
                           waits=[m_lg] + self.bank_w(1), inc_group=True)
            m_lgtr = self.V("tensor_copy", s["lgT"][:, c, :], ps[1][:, 0:8], waits=[t_tr])
            self.bank_done(1, m_lgtr)
            self.xln_mark = m_xln

        def SPC():
            self.V("tensor_copy", s["scr8"][:, 7, 0:128], s["c_ones32_sb"][:, :])

        lgw = s["lgT"][:, :, :]
        self.V("tensor_copy", s["scr8"][:, 7, 0:128], s["c_ones32_sb"][:, :],
               waits=[m_lgtr])
        m_lgtr = self.V("tensor_copy", s["scr8"][:, 7, 0:128], s["c_ones32_sb"][:, :])
        self.S("activation", s["gw_sc"][:, :, :], lgw, AF.Tanh, scale=0.5,
               waits=[m_lgtr])
        self.V("tensor_scalar", s["gw_sc"][:, :, :], s["gw_sc"][:, :, :], 0.5, 0.5,
               ALU.mult, ALU.add, waits=[("s", p.now("s"))])
        self.tap(f"fsc_b{batch}f{call}", s["gw_sc"][:, :, :], [("v", p.now("v"))])
        SPC()
        ebias3 = s["e_bias_sb"][:, :].unsqueeze(1).broadcast_to([128, NCHUNK, E_FFN])
        eidx3 = s["eidx_sb"][:, :].unsqueeze(1).broadcast_to([128, NCHUNK, E_FFN])
        self.V("tensor_tensor", s["gw_tmp"][:, :, :], s["gw_sc"][:, :, :], ebias3,
               ALU.add)
        self.tap(f"ftmp_b{batch}f{call}", s["gw_tmp"][:, :, :], [("v", p.now("v"))])
        SPC()
        self.V("tensor_tensor", s["gw_gs"][:, :, :], s["gw_tmp"][:, :, 0:8:2],
               s["gw_tmp"][:, :, 1:8:2], ALU.add)
        self.tap(f"fgs_b{batch}f{call}", s["gw_gs"][:, :, :], [("v", p.now("v"))])
        SPC()
        self.V("tensor_max", s["gw_h2"][:, :, :], s["gw_gs"][:, :, 0:2],
               s["gw_gs"][:, :, 2:4])
        SPC()
        self.V("tensor_max", s["gw_m1"][:, :, :], s["gw_h2"][:, :, 0:1],
               s["gw_h2"][:, :, 1:2])
        SPC()
        self.V("tensor_tensor", s["gw_t1"][:, :, :], s["gw_gs"][:, :, :],
               s["gw_m1"][:, :, :].broadcast_to([128, NCHUNK, 4]), ALU.is_ge)
        SPC()
        self.V("scalar_tensor_tensor", s["gw_g2"][:, :, :], s["gw_t1"][:, :, :], -1e9,
               s["gw_gs"][:, :, :], ALU.mult, ALU.add)
        SPC()
        self.V("tensor_max", s["gw_h2"][:, :, :], s["gw_g2"][:, :, 0:2],
               s["gw_g2"][:, :, 2:4])
        SPC()
        self.V("tensor_max", s["gw_m2"][:, :, :], s["gw_h2"][:, :, 0:1],
               s["gw_h2"][:, :, 1:2])
        SPC()
        self.V("tensor_tensor", s["gw_t2"][:, :, :], s["gw_g2"][:, :, :],
               s["gw_m2"][:, :, :].broadcast_to([128, NCHUNK, 4]), ALU.is_ge)
        SPC()
        self.V("tensor_add", s["gw_t1"][:, :, :], s["gw_t1"][:, :, :],
               s["gw_t2"][:, :, :])
        self.tap(f"gmask_b{batch}f{call}", s["gw_t1"][:, :, :], [("v", p.now("v"))])
        SPC()
        self.V("tensor_copy", s["gw_sm"][:, :, 0:8:2], s["gw_t1"][:, :, :])
        self.V("tensor_copy", s["gw_sm"][:, :, 1:8:2], s["gw_t1"][:, :, :])
        SPC()
        self.V("tensor_mul", s["gw_msk"][:, :, :], s["gw_tmp"][:, :, :],
               s["gw_sm"][:, :, :])
        SPC()
        self.V("tensor_scalar", s["gw_z"][:, :, :], s["gw_msk"][:, :, :], 0.0, None,
               ALU.is_equal)
        SPC()
        self.V("tensor_mul", s["gw_z"][:, :, :], s["gw_z"][:, :, :], eidx3)
        SPC()
        self.V("tensor_sub", s["gw_key"][:, :, :], s["gw_msk"][:, :, :],
               s["gw_z"][:, :, :])
        SPC()
        self.V("tensor_scalar", s["gw_sm"][:, :, :], s["gw_msk"][:, :, :], 0.0, None,
               ALU.is_lt)
        SPC()
        self.V("scalar_tensor_tensor", s["gw_key"][:, :, :], s["gw_sm"][:, :, :], -8.0,
               s["gw_key"][:, :, :], ALU.mult, ALU.add)
        SPC()
        self.V("tensor_max", s["gw_gs"][:, :, :], s["gw_key"][:, :, 0:4],
               s["gw_key"][:, :, 4:8])
        SPC()
        self.V("tensor_max", s["gw_h2"][:, :, :], s["gw_gs"][:, :, 0:2],
               s["gw_gs"][:, :, 2:4])
        SPC()
        self.V("tensor_max", s["gw_m1"][:, :, :], s["gw_h2"][:, :, 0:1],
               s["gw_h2"][:, :, 1:2])
        SPC()
        self.V("tensor_tensor", s["gw_sm"][:, :, :], s["gw_key"][:, :, :],
               s["gw_m1"][:, :, :].broadcast_to([128, NCHUNK, E_FFN]), ALU.is_ge)
        SPC()
        self.V("scalar_tensor_tensor", s["gw_z"][:, :, :], s["gw_sm"][:, :, :], -1e9,
               s["gw_key"][:, :, :], ALU.mult, ALU.add)
        SPC()
        self.V("tensor_max", s["gw_t1"][:, :, :], s["gw_z"][:, :, 0:4],
               s["gw_z"][:, :, 4:8])
        SPC()
        self.V("tensor_max", s["gw_h2"][:, :, :], s["gw_t1"][:, :, 0:2],
               s["gw_t1"][:, :, 2:4])
        SPC()
        self.V("tensor_max", s["gw_m2"][:, :, :], s["gw_h2"][:, :, 0:1],
               s["gw_h2"][:, :, 1:2])
        SPC()
        self.V("tensor_tensor", s["gw_tmp"][:, :, :], s["gw_z"][:, :, :],
               s["gw_m2"][:, :, :].broadcast_to([128, NCHUNK, E_FFN]), ALU.is_ge)
        SPC()
        self.V("tensor_add", s["gw_sm"][:, :, :], s["gw_sm"][:, :, :],
               s["gw_tmp"][:, :, :])
        self.tap(f"fsel_b{batch}f{call}", s["gw_sm"][:, :, :], [("v", p.now("v"))])
        SPC()
        self.V("tensor_mul", s["gw_msk"][:, :, :], s["gw_sm"][:, :, :],
               s["gw_sc"][:, :, :])
        SPC()
        self.V("tensor_add", s["gw_gs"][:, :, :], s["gw_msk"][:, :, 0:4],
               s["gw_msk"][:, :, 4:8])
        SPC()
        self.V("tensor_add", s["gw_h2"][:, :, :], s["gw_gs"][:, :, 0:2],
               s["gw_gs"][:, :, 2:4])
        SPC()
        self.V("tensor_add", s["gw_m1"][:, :, :], s["gw_h2"][:, :, 0:1],
               s["gw_h2"][:, :, 1:2])
        SPC()
        self.V("tensor_scalar_add", s["gw_se"][:, :, :], s["gw_m1"][:, :, :], 1e-20)
        SPC()
        self.V("reciprocal", s["gw_rs"][:, :, :], s["gw_se"][:, :, :])
        SPC()
        m_cb = self.V("tensor_tensor", s["cb_tm"][:, :, :], s["gw_msk"][:, :, :],
                      s["gw_rs"][:, :, :].broadcast_to([128, NCHUNK, E_FFN]), ALU.mult)
        cb_mark = m_cb
        self.tap(f"cbtm_b{batch}f{call}", s["cb_tm"][:, :, :], [cb_mark])
        self.tap(f"lgT_b{batch}f{call}", s["lgT"][:, :, :], [cb_mark])
        self.tap(f"xlnf_b{batch}f{call}", s["x_ln"][:, :, 0:256], [cb_mark])
        self.tap(f"xlna_b{batch}f{call}", s["x_ln"][:, :, :], [cb_mark])

        MACRO = 1024
        TILE = 512
        fy = s["f_y"]
        m_res = None
        for mi in range(T // MACRO):
            mt0 = mi * MACRO

            def emit_staging(mi=mi, mt0=mt0):
                m_cfm = None
                for cc in range(mt0 // L, (mt0 + MACRO) // L):
                    off = (cc - mt0 // L) * 128
                    self.V("tensor_copy", s["gw_stg"][:, :],
                           s["cb_tm"][:, cc, :],
                           waits=[cb_mark, getattr(self, "cbfm_cons", ("t", 0))])
                    m_c16 = self.V("tensor_copy", s["scr8"][:, 7, 0:128],
                                   s["c_ones32_sb"][:, :])
                    t_ctr = self.TR(ps[5].bitcast(BF16)[0:8, 0:128],
                                    s["gw_stg"][:, :], s["c_id16_sb"][:],
                                    waits=[m_c16] + self.bank_w(5), inc_group=True)
                    m_cfm = self.V("tensor_copy", s["cb_fm16"][0:8, off:off + 128],
                                   ps[5].bitcast(BF16)[0:8, 0:128], waits=[t_ctr])
                    self.bank_done(5, m_cfm)
                return m_cfm

            for e in range(E_FFN):
                slot = s["w_ffn0"] if e % 2 == 0 else s["w_ffn1"]
                for ti in range(MACRO // TILE):
                    t0 = mt0 + ti * TILE
                    for m in range(4):
                        for kk in range(NCH):
                            t_g = self.MM(ps[m][:, 0:TILE],
                                          slot[:, kk * 512 + m * 128:kk * 512 + (m + 1) * 128],
                                          s["x_ln"][:, kk, t0:t0 + TILE],
                                          start=(kk == 0), stop=(kk == NCH - 1),
                                          waits=([eld[e], self.xln_mark]
                                                 + self.bank_w(0) + self.bank_w(1)
                                                 + self.bank_w(2) + self.bank_w(3))
                                          if (m == 0 and kk == 0) else (),
                                          inc_group=(kk == NCH - 1))
                        m_sg = self.S("activation", s["f_sg"][:, m, :],
                                      ps[m][:, 0:TILE], AF.Silu, waits=[t_g])
                        self.bank_done(m, m_sg)
                    if e == 0 and ti == 0:
                        m_cfm = emit_staging()
                    t_bc = self.MM(ps[4][:, 0:TILE],
                                   s["sel16_sb"][:, e * 128:(e + 1) * 128],
                                   s["cb_fm16"][0:8, ti * TILE:(ti + 1) * TILE],
                                   start=True, stop=True,
                                   waits=[m_cfm] + self.bank_w(4), inc_group=True)
                    m_cbb = self.V("tensor_copy", s["cb_bc"][:, ti % 2, :],
                                   ps[4][:, 0:TILE],
                                   waits=[t_bc, getattr(self, "cbb_cons", ("v", 0))])
                    self.bank_done(4, m_cbb)
                    self.cbfm_cons = t_bc
                    m_h2 = None
                    for m in range(4):
                        for kk in range(NCH):
                            t_u = self.MM(ps[m][:, 0:TILE],
                                          slot[:, 4096 + kk * 512 + m * 128:4096 + kk * 512 + (m + 1) * 128],
                                          s["x_ln"][:, kk, t0:t0 + TILE],
                                          start=(kk == 0), stop=(kk == NCH - 1),
                                          waits=self.bank_w(m) if kk == 0 else (),
                                          inc_group=(kk == NCH - 1))
                        self.V("tensor_tensor", s["f_sg"][:, m, :],
                               s["f_sg"][:, m, :], ps[m][:, 0:TILE],
                               ALU.mult, waits=[t_u])
                        m_h = self.V("tensor_tensor", s["f_sg"][:, m, :],
                                     s["f_sg"][:, m, :],
                                     s["cb_bc"][:, ti % 2, :], ALU.mult,
                                     waits=[m_cbb] if m == 0 else ())
                        self.bank_done(m, m_h)
                        m_h2 = m_h
                    m_y = None
                    for m in range(8):
                        bank = 4 + (m % 4)
                        for ji in range(4):
                            t_d = self.MM(ps[bank][:, 0:TILE],
                                          slot[:, 8192 + ji * 1024 + m * 128:8192 + ji * 1024 + (m + 1) * 128],
                                          s["f_sg"][:, ji, :],
                                          start=(ji == 0), stop=(ji == 3),
                                          waits=([m_h2] + self.bank_w(bank))
                                          if ji == 0 else (),
                                          inc_group=(ji == 3))
                        dst = fy[:, m, (t0 - mt0):(t0 - mt0) + TILE]
                        if e == 0:
                            m_y = self.V("tensor_copy", dst, ps[bank][:, 0:TILE],
                                         waits=[t_d])
                        else:
                            m_y = self.V("tensor_add", dst, dst, ps[bank][:, 0:TILE],
                                         waits=[t_d])
                        self.bank_done(bank, m_y)
                    self.cbb_cons = m_h2
                    slot_use[e % 2] = t_d
                if mi == 0:
                    self.tap(f"fy_e{e}_b{batch}f{call}", s["f_y"][:, :, :], [m_y])
                pos = mi * E_FFN + e
                nxt = pos + 2
                if nxt < (T // MACRO) * E_FFN:
                    e2 = nxt % E_FFN
                    eld[e2] = self.load_expert(e2, [slot_use[e % 2]])
            for jj in range(4):
                m_res = self.V("tensor_add",
                               s["resid"][:, 2 * jj:2 * jj + 2, mt0:mt0 + MACRO],
                               s["resid"][:, 2 * jj:2 * jj + 2, mt0:mt0 + MACRO],
                               fy[:, 2 * jj:2 * jj + 2, :],
                               waits=[m_y] if jj == 0 else ())
        self.attA = m_res
        self.ffn_done = self.phase_mark()


    def gen_attn_b(self, batch):
        p, s, ps, d = self.p, self.s, self.ps, self.d
        pm = self.phase_mark()
        TQ = 512
        NT = T // TQ
        if not hasattr(self, "exps_cons"):
            self.exps_cons = {}

        p.rec("gpsimd", "dma_start", out=s["wb_o"][:, :, :],
              in_=d["wo_b"][:, :].rearrange("(j p) o -> p j o", p=128),
              waits=pm, inc="g", incn=16)
        m_wo = ("g", p.now("g"))

        def load_qkv(hg, waits):
            c0 = hg * 256
            for i, nm in enumerate(("wb_q", "wb_k", "wb_v")):
                p.rec("gpsimd", "dma_start", out=s[nm][:, :, :],
                      in_=d["wqkvT_b"][i, :, c0:c0 + 256].rearrange(
                          "(j p) o -> p j o", p=128),
                      waits=waits if i == 0 else (), inc="g", incn=16)
            return ("g", p.now("g"))

        m_qkvw = load_qkv(0, pm)

        m_xln = None
        for c in range(NCHUNK):
            t0 = c * L
            m_xln = self._ln_chunk_to(s["x_ln"][:, :, t0:t0 + L],
                                      s["resid"][:, :, t0:t0 + L],
                                      pm if c == 0 else (), t0)

        m_shift = ("y", p.now("y"))
        last_att = ("v", p.now("v"))
        for hg in range(4):
            m_qk = None
            for wname, dstname, bcol in (("wb_q", "qf_b", 0), ("wb_k", "kf_b", 1)):
                for mj in range(2):
                    for tt in range(NT):
                        t0 = tt * TQ
                        bank = (tt + mj * NT) % 2
                        for j in range(NCH):
                            t_p = self.MM(ps[bank][:, 0:TQ],
                                          s[wname][:, j, mj * 128:(mj + 1) * 128],
                                          s["x_ln"][:, j, t0:t0 + TQ],
                                          start=(j == 0), stop=(j == NCH - 1),
                                          waits=([m_qkvw, m_xln]
                                                 + self.bank_w(bank))
                                          if j == 0 else (),
                                          inc_group=(j == NCH - 1))
                        m_qk = self.V("tensor_scalar",
                                      s[dstname][:, mj, t0:t0 + TQ],
                                      ps[bank][:, 0:TQ],
                                      s["bqk_sb"][:, 2 * hg + mj, bcol:bcol + 1],
                                      None, ALU.add, waits=[t_p])
                        self.bank_done(bank, m_qk)
            if hg == 0:
                self.tap(f"bqf_b{batch}", s["qf_b"][:, :, :], [m_qk])
                self.tap(f"bkf_b{batch}", s["kf_b"][:, :, :], [m_qk])
            m_vt = None
            for tc in range(NCHUNK):
                t0 = tc * L
                bank = 2 + tc % 2
                for j in range(NCH):
                    t_v = self.MM(ps[bank][:, 0:256],
                                  s["x_ln"][:, j, t0:t0 + L],
                                  s["wb_v"][:, j, 0:256],
                                  start=(j == 0), stop=(j == NCH - 1),
                                  waits=([m_qkvw, m_xln] + self.bank_w(bank))
                                  if j == 0 else (),
                                  inc_group=(j == NCH - 1))
                for hh in range(4):
                    self.V("tensor_tensor",
                           s["vt_b"][:, tc, hh * 65:hh * 65 + 64],
                           ps[bank][:, hh * 64:(hh + 1) * 64],
                           s["bv_bc_sb"][:, hg * 256 + hh * 64:
                                         hg * 256 + (hh + 1) * 64],
                           ALU.add, waits=[t_v] if hh == 0 else ())
                m_vt = self.V("tensor_copy", s["vt_b"][:, tc, 64:260:65],
                              s["c_ones32_sb"][:, 0:4])
                self.bank_done(bank, m_vt)
            if hg == 0:
                self.tap(f"bvt_b{batch}", s["vt_b"][:, :, :], [m_vt])
            if hg < 3:
                nxt_qkvw = load_qkv(hg + 1, [m_qk, m_vt])

            for mj in range(2):
                slot = 2 * hg + mj
                for tt in range(NT):
                    tq0 = tt * TQ
                    tp = (mj * NT + tt) % 2
                    avA = 4 if tp == 0 else 6
                    avB = 5 if tp == 0 else 7
                    t_avA = t_avB = None

                    def emit_scores(tkc, first):
                        tk0 = tkc * L
                        bA = tkc % 2
                        bB = 2 + tkc % 2
                        wA = self.bank_w(bA)
                        if first:
                            wA = wA + [m_qk, m_vt]
                        t_sA = self.MM(ps[bA][:, 0:TQ],
                                       s["kf_b"][0:64, mj, tk0:tk0 + L],
                                       s["qf_b"][0:64, mj, tq0:tq0 + TQ],
                                       start=True, stop=True,
                                       tile_position=(0, 0),
                                       waits=wA, inc_group=True)
                        t_sB = self.MM(ps[bB][:, 0:TQ],
                                       s["kf_b"][64:128, mj, tk0:tk0 + L],
                                       s["qf_b"][64:128, mj, tq0:tq0 + TQ],
                                       start=True, stop=True,
                                       tile_position=(64, 0),
                                       waits=self.bank_w(bB), inc_group=True)
                        return t_sA, t_sB

                    pend = [emit_scores(0, tt == 0 and mj == 0),
                            emit_scores(1, False)]
                    for tkc in range(NCHUNK):
                        bA = tkc % 2
                        bB = 2 + tkc % 2
                        t_sA, t_sB = pend[tkc % 2]
                        eA = s["exps"][:, (tkc % 2) * 4:(tkc % 2) * 4 + 2, :]
                        eB = s["exps"][:, (tkc % 2) * 4 + 2:(tkc % 2) * 4 + 4, :]
                        m_eA = self.S("activation", eA,
                                      ps[bA][:, 0:TQ].rearrange(
                                          "p (a b) -> p a b", b=256),
                                      AF.Exp, bias=s["c_n30_sb"][:, 0:1],
                                      scale=0.125,
                                      waits=[t_sA, self.exps_cons.get(
                                          (tkc % 2, 0), ("t", 0))])
                        m_eB = self.S("activation", eB,
                                      ps[bB][:, 0:TQ].rearrange(
                                          "p (a b) -> p a b", b=256),
                                      AF.Exp, bias=s["c_n30_sb"][:, 0:1],
                                      scale=0.125,
                                      waits=[t_sB, self.exps_cons.get(
                                          (tkc % 2, 1), ("t", 0))])
                        self.bank_done(bA, m_eA)
                        self.bank_done(bB, m_eB)
                        if tkc + 2 < NCHUNK:
                            pend[tkc % 2] = emit_scores(tkc + 2, False)
                        t_avA = self.MM(ps[avA][0:65, 0:TQ],
                                        s["vt_b"][:, tkc,
                                                  (2 * mj) * 65:(2 * mj) * 65 + 65],
                                        eA.rearrange("p a b -> p (a b)"),
                                        start=(tkc == 0),
                                        stop=(tkc == NCHUNK - 1),
                                        waits=([m_eA] + self.bank_w(avA))
                                        if tkc == 0 else [m_eA],
                                        inc_group=(tkc == NCHUNK - 1))
                        t_avB = self.MM(ps[avB][0:65, 0:TQ],
                                        s["vt_b"][:, tkc,
                                                  (2 * mj + 1) * 65:
                                                  (2 * mj + 1) * 65 + 65],
                                        eB.rearrange("p a b -> p (a b)"),
                                        start=(tkc == 0),
                                        stop=(tkc == NCHUNK - 1),
                                        waits=([m_eB] + self.bank_w(avB))
                                        if tkc == 0 else [m_eB],
                                        inc_group=(tkc == NCHUNK - 1))
                        self.exps_cons[(tkc % 2, 0)] = t_avA
                        self.exps_cons[(tkc % 2, 1)] = t_avB
                    m_zA = self.V("tensor_copy", s["b_rz"][64:65, 0:TQ],
                                  ps[avA][64:65, 0:TQ], waits=[t_avA])
                    t_bcA = self.MM(ps[0][0:64, 0:TQ],
                                    s["c_ones32_sb"][64:65, 0:64],
                                    s["b_rz"][64:65, 0:TQ],
                                    start=True, stop=True,
                                    waits=[m_zA] + self.bank_w(0),
                                    inc_group=True)
                    m_rcA = self.V("reciprocal", s["b_rzbc"][0:64, 0:TQ],
                                   ps[0][0:64, 0:TQ], waits=[t_bcA])
                    self.bank_done(0, m_rcA)
                    m_aA = self.V("tensor_mul",
                                  s["att_b"][0:64, slot, tq0:tq0 + TQ],
                                  ps[avA][0:64, 0:TQ], s["b_rzbc"][0:64, 0:TQ])
                    m_zB = self.V("tensor_copy", s["b_rz"][64:65, 0:TQ],
                                  ps[avB][64:65, 0:TQ], waits=[t_avB, t_bcA])
                    t_bcB = self.MM(ps[1][0:64, 0:TQ],
                                    s["c_ones32_sb"][64:65, 0:64],
                                    s["b_rz"][64:65, 0:TQ],
                                    start=True, stop=True,
                                    waits=[m_zB] + self.bank_w(1),
                                    inc_group=True)
                    m_rcB = self.V("reciprocal", s["b_rzbc"][0:64, 0:TQ],
                                   ps[1][0:64, 0:TQ], waits=[t_bcB])
                    self.bank_done(1, m_rcB)
                    m_aB = self.V("tensor_mul", s["att_tmp"][0:64, 0:TQ],
                                  ps[avB][0:64, 0:TQ], s["b_rzbc"][0:64, 0:TQ],
                                  waits=[m_shift])
                    self.bank_done(avA, m_aA)
                    self.bank_done(avB, m_aB)
                    p.rec("sync", "dma_start",
                          out=s["att_b"][64:128, slot, tq0:tq0 + TQ],
                          in_=s["att_tmp"][0:64, 0:TQ],
                          waits=[m_aB], inc="y", incn=16)
                    m_shift = ("y", p.now("y"))
                    last_att = m_aA
            m_qkvw = nxt_qkvw if hg < 3 else m_qkvw

        self.tap(f"batt_b{batch}", s["att_b"][:, :, :], [last_att])
        m_res = None
        for tt in range(NT):
            tq0 = tt * TQ
            for m in range(NCH):
                bank = (m + tt * NCH) % 2
                for j in range(NCH):
                    t_o = self.MM(ps[bank][:, 0:TQ],
                                  s["wb_o"][:, j, m * 128:(m + 1) * 128],
                                  s["att_b"][:, j, tq0:tq0 + TQ],
                                  start=(j == 0), stop=(j == NCH - 1),
                                  waits=([m_wo, last_att, m_shift]
                                         + self.bank_w(bank))
                                  if j == 0 else (),
                                  inc_group=(j == NCH - 1))
                self.V("tensor_scalar", s["b_rz"][:, 0:TQ], ps[bank][:, 0:TQ],
                       s["bo_sb"][:, m, 0:1], None, ALU.add, waits=[t_o])
                m_res = self.V("tensor_add", s["resid"][:, m, tq0:tq0 + TQ],
                               s["resid"][:, m, tq0:tq0 + TQ],
                               s["b_rz"][:, 0:TQ])
                self.bank_done(bank, m_res)
        self.attA = m_res

    def gen_store(self, batch):
        p, s, d = self.p, self.d if False else self.s, self.d
        s = self.s
        p.rec("sync", "dma_start", out=d["out"][batch].rearrange("(j p) t -> p j t", p=128),
              in_=s["resid"][:, :, :],
              waits=[self.attA, ("v", p.now("v"))], inc="y", incn=16)
        self.st = ("y", p.now("y"))

    def finalize(self):
        p = self.p
        p.rec("sync", "_wait_only", waits=[("y", p.now("y")), ("g", p.now("g"))])
        p.emit()
        return self.nc



def _prep_inputs(inputs):
    f32 = np.float32
    x = np.ascontiguousarray(np.asarray(inputs["L0_attn_a_x"], f32).transpose(0, 2, 1))
    per = {}
    per["wqT"] = np.ascontiguousarray(np.asarray(inputs["L0_attn_a_Wq"], f32).T)
    per["wkT"] = np.ascontiguousarray(np.asarray(inputs["L0_attn_a_Wk"], f32).T)
    per["wvT"] = np.ascontiguousarray(np.asarray(inputs["L0_attn_a_Wv"], f32).T)
    per["wgT"] = np.ascontiguousarray(np.asarray(inputs["L0_attn_a_Wg"], f32).T)
    per["woT"] = np.ascontiguousarray(np.asarray(inputs["L0_attn_a_Wo"], f32).T)
    a_small = np.stack([
        np.asarray(inputs["L0_attn_a_time_decay"], f32),
        np.asarray(inputs["L0_attn_a_mix_q"], f32).reshape(-1),
        np.asarray(inputs["L0_attn_a_mix_k"], f32).reshape(-1),
        np.asarray(inputs["L0_attn_a_mix_v"], f32).reshape(-1),
        np.asarray(inputs["L0_attn_a_mix_g"], f32).reshape(-1),
        np.asarray(inputs["L0_attn_a_norm_scale"], f32),
    ], axis=1)
    per["a_small"] = np.ascontiguousarray(a_small)
    gp = np.asarray(inputs["L0_ffn_gate_proj"], f32)
    up = np.asarray(inputs["L0_ffn_up_proj"], f32)
    dn = np.asarray(inputs["L0_ffn_down_proj"], f32)
    ffn_w = np.concatenate([
        gp.transpose(0, 2, 1).reshape(E_FFN, -1),
        up.transpose(0, 2, 1).reshape(E_FFN, -1),
        dn.transpose(0, 2, 1).reshape(E_FFN, -1),
    ], axis=1)
    per["ffn_w"] = np.ascontiguousarray(ffn_w)
    per["gate_wT"] = np.ascontiguousarray(np.asarray(inputs["L0_ffn_gate_weight"], f32).T)
    per["e_bias_bc"] = np.ascontiguousarray(
        np.tile(np.asarray(inputs["L0_ffn_e_score_correction_bias"], f32)[None, :], (128, 1)))
    per["eidx_bc"] = np.ascontiguousarray(
        np.tile(np.arange(E_FFN, dtype=f32)[None, :], (128, 1)))
    wqkv = np.asarray(inputs["L0_attn_b_W_qkv"], f32)
    bqkv = np.asarray(inputs["L0_attn_b_b_qkv"], f32)
    wq = wqkv[:, :, 0:DB].transpose(1, 0, 2).reshape(C, C)
    wk = wqkv[:, :, DB:2 * DB].transpose(1, 0, 2).reshape(C, C)
    wv = wqkv[:, :, 2 * DB:].transpose(1, 0, 2).reshape(C, C)
    per["wqkvT_b"] = np.ascontiguousarray(np.stack([wq, wk, wv]))
    per["wo_b"] = np.ascontiguousarray(np.asarray(inputs["L0_attn_b_W_o"], f32))
    per["bqkT"] = np.ascontiguousarray(np.stack(
        [bqkv[:, 0:DB].reshape(-1), bqkv[:, DB:2 * DB].reshape(-1)], axis=1))
    per["bv_bc"] = np.ascontiguousarray(np.tile(bqkv[:, 2 * DB:].reshape(-1)[None, :], (128, 1)))
    per["bo_b"] = np.ascontiguousarray(np.asarray(inputs["L0_attn_b_b_o"], f32)[:, None])
    return x, per


def _build(n_phases, n_batches, taps=()):
    b = Builder(n_phases, n_batches, taps)
    b.gen_consts()
    b.gen_warmup()
    for batch in range(n_batches):
        b.gen_load_x(batch, waits=[("y", b.p.now("y"))])
        b.gen_tables()
        b.gen_attn_a_weights(waits=b.phase_mark())
        b.gen_attn_a(batch)
        if n_phases >= 2:
            b.gen_ffn(batch, 1)
        if n_phases >= 3:
            b.gen_attn_b(batch)
        if n_phases >= 4:
            b.gen_ffn(batch, 2)
        b.gen_store(batch)
    return b.finalize()


_CACHE = {}


def kernel(**inputs):
    x, per = _prep_inputs(inputs)
    key = (N_PHASES, NB)
    if key not in _CACHE:
        _CACHE[key] = _build(*key)
    nc = _CACHE[key]
    in_maps = []
    for core in range(8):
        m = dict(per)
        m["x"] = np.ascontiguousarray(x[2 * core:2 * core + 2])
        in_maps.append(m)
    res = run_bass_kernel_spmd(nc, in_maps, core_ids=list(range(8)))
    outs = []
    for core in range(8):
        o = np.asarray(res.results[core]["out"], np.float32)
        outs.append(o.transpose(0, 2, 1))
    return np.concatenate(outs, axis=0)
